# version 01
# baseline (speedup 1.0000x reference)
import contextlib
import math
import numpy as np
import ml_dtypes
import concourse.bass as bass
import concourse.mybir as mybir
from concourse.bass_utils import run_bass_kernel_spmd

F32 = mybir.dt.float32
BF16 = mybir.dt.bfloat16
AF = mybir.ActivationFunctionType
ALU = mybir.AluOpType
AX = mybir.AxisListType

OFFS = ((0, 3, 4, 7), (1, 2, 5, 6))
D = 4096
NOWN = 2048
NLOC = 4096
QSCALE = 1.0 / math.sqrt(128.0)
ENGS = ("te", "act", "dve", "pool", "sync")


class Sched:

    def __init__(self):
        self.events = []
        self.lastw = {}
        self.readers = {}
        self.chans = set()

    def ev(self, eng, fn, reads=(), writes=(), dma=False, force_inc=False, chan=None):
        deps = set()
        for r in reads:
            w = self.lastw.get(r)
            if w is not None:
                deps.add(w)
        for w_ in writes:
            lw = self.lastw.get(w_)
            if lw is not None:
                deps.add(lw)
            for rd in self.readers.get(w_, ()):
                deps.add(rd)
        if dma:
            assert chan is not None
            self.chans.add(chan)
        key = chan if dma else eng
        i = len(self.events)
        self.events.append([eng, fn, deps, dma, force_inc, key])
        for r in reads:
            self.readers.setdefault(r, set()).add(i)
        for w_ in writes:
            self.lastw[w_] = i
            self.readers[w_] = set()
        return i

    def finalize(self, nc, blk, ctx):
        n = len(self.events)
        keys = list(ENGS) + sorted(self.chans)
        sems = {k: ctx.enter_context(nc.semaphore(f"s{j}")) for j, k in enumerate(keys)}
        waits = []
        needed = set()
        for i, (eng, fn, deps, dma, forced, key) in enumerate(self.events):
            bykey = {}
            for d in deps:
                dk = self.events[d][5]
                if dk == "te" and key == "te":
                    continue
                v = bykey.get(dk)
                if v is None or d > v:
                    bykey[dk] = d
            waits.append(bykey)
            for d in bykey.values():
                needed.add(d)
            if forced:
                needed.add(i)
        val_after = [0] * n
        cum = {k: 0 for k in keys}
        for i, (eng, fn, deps, dma, forced, key) in enumerate(self.events):
            if i in needed:
                cum[key] += 16 if dma else 1
            val_after[i] = cum[key]
        per_eng = {e: [] for e in ENGS}
        for i, e in enumerate(self.events):
            per_eng[e[0]].append(i)
        events = self.events

        def mk(e):
            idxs = per_eng[e]

            def emit(engobj):
                water = {x: 0 for x in keys}
                for i in idxs:
                    _, fn, _, dma, _, key = events[i]
                    for dk, d in sorted(waits[i].items(), key=lambda kv: str(kv[0])):
                        v = val_after[d]
                        if v > water[dk]:
                            engobj.wait_ge(sems[dk], v)
                            water[dk] = v
                    inst = fn(engobj)
                    if i in needed:
                        inst.then_inc(sems[key], 16 if dma else 1)
            return emit

        blk.tensor(mk("te"))
        blk.scalar(mk("act"))
        blk.vector(mk("dve"))
        blk.gpsimd(mk("pool"))
        blk.sync(mk("sync"))


def build(phases=("f1a", "f1b", "qa", "qb", "at", "f3"), dbg=None):
    nc = bass.Bass("TRN2", target_bir_lowering=False)

    x = nc.dram_tensor("x", [D, NLOC], F32, kind="ExternalInput").ap()
    wbB = nc.dram_tensor("wbB", [32, 128, 4224], F32, kind="ExternalInput").ap()
    wq4 = nc.dram_tensor("wq4", [32, 128, 4096], F32, kind="ExternalInput").ap()
    wk4 = nc.dram_tensor("wk4", [8, 128, 4096], F32, kind="ExternalInput").ap()
    wv4 = nc.dram_tensor("wv4", [8, 128, 4096], F32, kind="ExternalInput").ap()
    wo4 = nc.dram_tensor("wo4", [32, 128, 4096], F32, kind="ExternalInput").ap()
    gtm = nc.dram_tensor("gtm", [4096, 40], F32, kind="ExternalInput").ap()
    awm = nc.dram_tensor("awm", [128, 4096], F32, kind="ExternalInput").ap()
    bbm = nc.dram_tensor("bbm", [128, 32], F32, kind="ExternalInput").ap()
    bthr = nc.dram_tensor("bthr", [1, 1], F32, kind="ExternalInput").ap()
    maskm = nc.dram_tensor("maskm", [8, 128, 512], F32, kind="ExternalInput").ap()
    idenm = nc.dram_tensor("idenm", [128, 128], F32, kind="ExternalInput").ap()
    onesrm = nc.dram_tensor("onesrm", [1, 128], F32, kind="ExternalInput").ap()
    onescm = nc.dram_tensor("onescm", [128, 1], F32, kind="ExternalInput").ap()
    e16m = nc.dram_tensor("e16m", [8, 128], F32, kind="ExternalInput").ap()

    out = nc.dram_tensor("out", [D, NOWN], F32, kind="ExternalOutput").ap()

    x1s = nc.dram_tensor("x1s", [D, NLOC], F32, kind="Internal").ap()
    x2s = nc.dram_tensor("x2s", [D, NOWN], F32, kind="Internal").ap()
    qs = nc.dram_tensor("qs", [D, NOWN], BF16, kind="Internal").ap()
    ks = nc.dram_tensor("ks", [1024, NLOC], BF16, kind="Internal").ap()
    vs = nc.dram_tensor("vs", [NLOC, 1024], BF16, kind="Internal").ap()
    vs3 = vs.rearrange("(blk pp) c -> pp blk c", pp=128)

    dbg_out = None
    if dbg is not None:
        shp, dt = {"x1s": ([D, NLOC], F32), "x2s": ([D, NOWN], F32),
                   "qs": ([D, NOWN], BF16), "ks": ([1024, NLOC], BF16),
                   "vs": ([NLOC, 1024], BF16)}[dbg]
        dbg_out = nc.dram_tensor("dbg", shp, dt, kind="ExternalOutput").ap()

    with contextlib.ExitStack() as ctx:
        E = ctx.enter_context
        sb = nc.sbuf_tensor
        xnr = E(sb("xnr", [128, 65536], BF16))
        wf = [E(sb(f"wf{i}", [128, 2112], F32)) for i in range(2)]
        wb = [E(sb(f"wb{i}", [128, 4224], BF16)) for i in range(2)]
        qg = E(sb("qg", [128, 2048], BF16))
        mbuf = E(sb("mbuf", [128, 4096], BF16))
        ev = [E(sb(f"ev{i}", [128, 512], F32)) for i in range(4)]
        pt = [E(sb(f"pt{i}", [128, 512], BF16)) for i in range(4)]
        stats = E(sb("stats", [1, 3072], F32))
        gtf = E(sb("gtf", [40, 512], F32))
        sgt = E(sb("sgt", [128, 40], F32))
        gsm = E(sb("gsm", [128, 16], F32))
        e8f = E(sb("e8f", [128, 8], F32))
        e8b = E(sb("e8b", [128, 8], F32))
        w8f = E(sb("w8f", [128, 8], F32))
        wnT = E(sb("wnT", [8, 2048], BF16))
        iden = E(sb("iden", [128, 128], F32))
        onesr = E(sb("onesr", [1, 128], F32))
        onesc32 = E(sb("onesc32", [128, 1], F32))
        onesc = E(sb("onesc", [128, 1], BF16))
        e16b = E(sb("e16b", [8, 128], BF16))
        e16f = E(sb("e16f", [8, 128], F32))
        bbh = E(sb("bbh", [128, 32], F32))
        bthrb = E(sb("bthrb", [128, 1], F32))
        negM = E(sb("negM", [128, 1], F32))
        p = [E(nc.psum_tensor(f"p{i}", [128, 512], F32)) for i in range(8)]
        blk = E(nc.Block())

        sch = Sched()
        S = sch.ev

        def st(i):
            return stats[0:1, i * 512:(i + 1) * 512]

        S("sync", lambda e: e.dma_start(out=iden[:], in_=idenm), writes=[("iden",)], dma=True, chan="c0")
        S("sync", lambda e: e.dma_start(out=onesr[:], in_=onesrm), writes=[("onesr",)], dma=True, chan="c1")
        S("sync", lambda e: e.dma_start(out=onesc32[:], in_=onescm), writes=[("onesc32",)], dma=True, chan="c2")
        S("sync", lambda e: e.dma_start(out=e16f[:], in_=e16m), writes=[("e16f",)], dma=True, chan="c3")
        S("sync", lambda e: e.dma_start(out=ev[1][:, 0:32], in_=bbm), writes=[("ev", 1)], dma=True, chan="c4")
        S("sync", lambda e: e.dma_start(out=gsm[0:1, 0:1], in_=bthr), writes=[("g", 0)], dma=True, chan="c5")
        S("pool", lambda e: e.tensor_copy(out=onesc[:], in_=onesc32[:]),
          reads=[("onesc32",)], writes=[("onesc",)])
        S("pool", lambda e: e.tensor_copy(out=e16b[:], in_=e16f[:]),
          reads=[("e16f",)], writes=[("e16b",)])
        S("pool", lambda e: e.tensor_scalar_mul(bbh[:], ev[1][:, 0:32], 0.5),
          reads=[("ev", 1)], writes=[("bbh",)])
        S("te", lambda e: e.matmul(p[7][:, 0:1], onesr[:], gsm[0:1, 0:1], start=True, stop=True),
          reads=[("g", 0), ("onesr",)], writes=[("p", 7)])
        S("dve", lambda e: e.tensor_copy(out=bthrb[:], in_=p[7][:, 0:1]),
          reads=[("p", 7)], writes=[("bthrb",)])

        def xnslice(ts, dc):
            c = dc * 2048 + ts * 512
            return xnr[:, c:c + 512]

        XNT = [("xnrt", t) for t in range(4)]

        def emit_pre(src_res, src_ap, col_base):
            src3 = src_ap.rearrange("(dc pp) t -> pp dc t", pp=128)
            xnr3 = xnr[:].rearrange("pp (dc t) -> pp dc t", t=2048)
            for s in range(32):
                ts = s // 8
                c0 = col_base + s * 64
                wfP = wf[s % 2]
                w3o = wfP[:, 0:2048].rearrange("pp (dc t) -> pp dc t", t=64)
                S("sync", (lambda e, o=w3o, c=c0: e.dma_start(out=o, in_=src3[:, :, c:c + 64])),
                  reads=list(src_res), writes=[("wf", s % 2)], dma=True, chan=f"cwf{s % 2}")
                cc = ts * 512 + (s % 8) * 64
                o3 = xnr3[:, :, cc:cc + 64]
                S("act", (lambda e, a=o3, b=w3o: e.activation(out=a, in_=b, func=AF.Copy)),
                  reads=[("wf", s % 2)], writes=[("xnrt", ts)])
            for ts in range(4):
                for dc in range(32):
                    sqb = pt[dc % 2]
                    S("act", (lambda e, o=sqb, i=xnslice(ts, dc): e.activation(
                        out=o[:], in_=i, func=AF.Square)),
                        reads=[("xnrt", ts)], writes=[("pt", dc % 2)])

                    def mm2(e, i=xnslice(ts, dc), q=sqb, s0=(dc == 0), s1=(dc == 31)):
                        e.matmul(p[0][0:1, :], onesc[:], i, start=s0, stop=s1)
                        return e.matmul(p[1][0:1, :], onesc[:], q[:], start=s0, stop=s1)
                    S("te", mm2, reads=[("xnrt", ts), ("pt", dc % 2), ("onesc",)],
                      writes=[("p", 0), ("p", 1)])
                S("dve", lambda e: e.tensor_scalar(st(0), p[0][0:1, :], 1.0 / 4096, None, ALU.mult),
                  reads=[("p", 0)], writes=[("st", 0)])
                S("dve", lambda e: e.tensor_scalar(st(1), p[1][0:1, :], 1.0 / 4096, None, ALU.mult),
                  reads=[("p", 1)], writes=[("st", 1)])
                S("dve", lambda e: e.tensor_mul(st(2), st(0), st(0)),
                  reads=[("st", 0)], writes=[("st", 2)])
                S("dve", lambda e: e.scalar_tensor_tensor(
                    out=st(3), in0=st(1), scalar=1e-5, in1=st(2),
                    op0=ALU.add, op1=ALU.subtract),
                  reads=[("st", 1), ("st", 2)], writes=[("st", 3)])
                S("act", lambda e: e.activation(out=st(4), in_=st(3), func=AF.Sqrt),
                  reads=[("st", 3)], writes=[("st", 4)])
                S("dve", lambda e: e.reciprocal(st(5), st(4)),
                  reads=[("st", 4)], writes=[("st", 5)])
                S("dve", lambda e: e.tensor_mul(st(2), st(0), st(5)),
                  reads=[("st", 0), ("st", 5)], writes=[("st", 2)])
                S("te", lambda e: e.matmul(p[2][:], onesr[:], st(5), start=True, stop=True),
                  reads=[("st", 5), ("onesr",)], writes=[("p", 2)])
                S("te", lambda e: e.matmul(p[3][:], onesr[:], st(2), start=True, stop=True),
                  reads=[("st", 2), ("onesr",)], writes=[("p", 3)])
                S("dve", lambda e: e.tensor_copy(out=ev[2][:], in_=p[2][:]),
                  reads=[("p", 2)], writes=[("ev", 2)])
                S("dve", lambda e: e.tensor_copy(out=ev[3][:], in_=p[3][:]),
                  reads=[("p", 3)], writes=[("ev", 3)])
                for dc in range(32):
                    evP = ev[dc % 2]
                    S("dve", (lambda e, o=evP, i=xnslice(ts, dc): e.tensor_mul(o[:], i, ev[2][:])),
                        reads=[("xnrt", ts), ("ev", 2)], writes=[("ev", dc % 2)])
                    S("pool", (lambda e, o=xnslice(ts, dc), i=evP: e.tensor_sub(o, i[:], ev[3][:])),
                        reads=[("ev", dc % 2), ("ev", 3)], writes=[("xnrt", ts)])

        def stream_chunk(src_getter, ci, scale=None, split_B=False):
            P = ci % 2
            wbP = wb[P]
            for h in range(2):
                wfH = wf[h]
                S("sync", (lambda e, a=src_getter(h), w=wfH: e.dma_start(out=w[:, 0:a.shape[1]], in_=a)),
                  writes=[("wf", h)], dma=True, chan=f"cwf{h}")
                if split_B and h == 1:
                    def cst(e, w=wfH, t=wbP):
                        e.tensor_scalar_mul(t[:, 2112:4096], w[:, 0:1984], 0.5)
                        return e.tensor_copy(out=t[:, 4096:4224], in_=w[:, 1984:2112])
                    S("pool", cst, reads=[("wf", h)], writes=[("wbh", P, h)])
                elif split_B and h == 0:
                    S("pool", (lambda e, w=wfH, t=wbP: e.tensor_scalar_mul(t[:, 0:2112], w[:, 0:2112], 0.5)),
                      reads=[("wf", h)], writes=[("wbh", P, h)])
                elif scale is not None:
                    S("pool", (lambda e, w=wfH, t=wbP, hh=h: e.tensor_scalar_mul(
                        t[:, hh * 2048:(hh + 1) * 2048], w[:, 0:2048], scale)),
                        reads=[("wf", h)], writes=[("wbh", P, h)])
                else:
                    S("pool", (lambda e, w=wfH, t=wbP, hh=h: e.tensor_copy(
                        out=t[:, hh * 2048:(hh + 1) * 2048], in_=w[:, 0:2048])),
                        reads=[("wf", h)], writes=[("wbh", P, h)])
            return wbP, [("wbh", P, 0), ("wbh", P, 1)]

        def emit_gating():
            gt3i = gtm.rearrange("(dc pp) c -> pp dc c", pp=128)
            gt3o = wf[0][:, 0:1280].rearrange("pp (dc c) -> pp dc c", c=40)
            S("sync", lambda e: e.dma_start(out=gt3o, in_=gt3i), writes=[("wf", 0)], dma=True, chan="cwf0")
            S("pool", lambda e: e.tensor_copy(out=wb[0][:, 0:1280], in_=wf[0][:, 0:1280]),
              reads=[("wf", 0)], writes=[("wbh", 0, 0), ("wbh", 0, 1)])

            def gtloop(e):
                for dc in range(32):
                    lh = wb[0][:, dc * 40:(dc + 1) * 40]
                    for ts in range(4):
                        i = e.matmul(p[ts][0:40, :], lh, xnslice(ts, dc),
                                     start=(dc == 0), stop=(dc == 31))
                return i
            S("te", gtloop, reads=[("wbh", 0, 0), ("wbh", 0, 1)] + XNT,
              writes=[("p", 0), ("p", 1), ("p", 2), ("p", 3)])
            S("sync", lambda e: e.dma_start(out=wf[1][:, 0:2048], in_=awm[:, 0:2048]),
              writes=[("wf", 1)], dma=True, chan="cwf1")
            S("pool", lambda e: e.tensor_copy(out=wb[1][:, 0:2048], in_=wf[1][:, 0:2048]),
              reads=[("wf", 1)], writes=[("wbh", 1, 0)])
            S("sync", lambda e: e.dma_start(out=wf[1][:, 0:2048], in_=awm[:, 2048:4096]),
              writes=[("wf", 1)], dma=True, chan="cwf1")
            S("pool", lambda e: e.tensor_copy(out=wb[1][:, 2048:4096], in_=wf[1][:, 0:2048]),
              reads=[("wf", 1)], writes=[("wbh", 1, 1)])
            for ts in range(4):
                S("dve", (lambda e, t=ts: e.tensor_copy(out=gtf[:], in_=p[t][0:40, :])),
                  reads=[("p", ts)], writes=[("gtf", ts)])
                for q in range(4):
                    bank = 6 + (q % 2)
                    S("te", (lambda e, t=ts, qq=q, b=bank: e.transpose(
                        p[b][0:128, 0:40], gtf[0:40, qq * 128:(qq + 1) * 128], iden[0:40, 0:40])),
                        reads=[("gtf", ts), ("iden",)], writes=[("p", bank)])
                    S("dve", (lambda e, b=bank: e.tensor_copy(out=sgt[:], in_=p[b][0:128, 0:40])),
                        reads=[("p", bank)], writes=[("sgt",)])
                    S("dve", lambda e: e.reduce_max(gsm[:, 0:1], sgt[:, 0:8], axis=AX.X),
                      reads=[("sgt",)], writes=[("g", 0)])
                    S("dve", lambda e: e.tensor_scalar_mul(gsm[:, 1:2], gsm[:, 0:1], -1.0),
                      reads=[("g", 0)], writes=[("g", 1)])
                    S("act", lambda e: e.activation(out=e8f[:], in_=sgt[:, 0:8], func=AF.Exp,
                                                    bias=gsm[:, 1:2]),
                      reads=[("sgt",), ("g", 1)], writes=[("e8f",)])
                    S("dve", lambda e: e.reduce_sum(gsm[:, 2:3], e8f[:], axis=AX.X),
                      reads=[("e8f",)], writes=[("g", 2)])
                    S("dve", lambda e: e.reciprocal(gsm[:, 3:4], gsm[:, 2:3]),
                      reads=[("g", 2)], writes=[("g", 3)])
                    S("act", lambda e: e.activation(out=gsm[:, 4:5], in_=sgt[:, 32:33],
                                                    func=AF.Sigmoid, bias=bthrb[:]),
                      reads=[("sgt",), ("bthrb",)], writes=[("g", 4)])
                    S("dve", lambda e: e.tensor_scalar_mul(gsm[:, 5:6], gsm[:, 4:5], 0.125),
                      reads=[("g", 4)], writes=[("g", 5)])
                    S("dve", lambda e: e.tensor_scalar_mul(e8b[:], e8f[:], gsm[:, 3:4]),
                      reads=[("e8f",), ("g", 3)], writes=[("e8b",)])
                    S("dve", lambda e: e.tensor_scalar(w8f[:], e8b[:], gsm[:, 5:6], None, ALU.subtract),
                      reads=[("e8b",), ("g", 5)], writes=[("w8f",)])
                    S("dve", lambda e: e.tensor_scalar_max(e8f[:], w8f[:], 0.0),
                      reads=[("w8f",)], writes=[("e8f",)])
                    S("dve", lambda e: e.reduce_sum(gsm[:, 6:7], e8f[:], axis=AX.X),
                      reads=[("e8f",)], writes=[("g", 6)])
                    S("dve", lambda e: e.tensor_single_scalar(gsm[:, 7:8], gsm[:, 6:7], 0.0, ALU.is_equal),
                      reads=[("g", 6)], writes=[("g", 7)])
                    S("dve", lambda e: e.tensor_add(gsm[:, 8:9], gsm[:, 6:7], gsm[:, 7:8]),
                      reads=[("g", 6), ("g", 7)], writes=[("g", 8)])
                    S("dve", lambda e: e.reciprocal(gsm[:, 9:10], gsm[:, 8:9]),
                      reads=[("g", 8)], writes=[("g", 9)])
                    S("dve", lambda e: e.tensor_scalar_mul(w8f[:], e8f[:], gsm[:, 9:10]),
                      reads=[("e8f",), ("g", 9)], writes=[("w8f",)])
                    S("te", (lambda e, b=bank: e.transpose(p[b][0:8, 0:128], w8f[:, 0:8], iden[:])),
                      reads=[("w8f",), ("iden",)], writes=[("p", bank)])
                    S("dve", (lambda e, t=ts, qq=q, b=bank: e.tensor_copy(
                        out=wnT[0:8, t * 512 + qq * 128: t * 512 + (qq + 1) * 128],
                        in_=p[b][0:8, 0:128])),
                        reads=[("p", bank)], writes=[("wnT", ts)])

            def hfloop(e):
                for dc in range(32):
                    lh = wb[1][:, dc * 128:(dc + 1) * 128]
                    for ts in range(4):
                        i = e.matmul(p[ts][:], lh, xnslice(ts, dc),
                                     start=(dc == 0), stop=(dc == 31))
                return i
            S("te", hfloop,
              reads=[("wbh", 1, 0), ("wbh", 1, 1)] + XNT + [("gtf", t) for t in range(4)],
              writes=[("p", 0), ("p", 1), ("p", 2), ("p", 3)])
            for ts in range(4):
                S("te", (lambda e, t=ts: e.matmul(p[5][:], e16b[:], wnT[0:8, t * 512:(t + 1) * 512],
                                                  start=True, stop=True)),
                  reads=[("wnT", ts), ("e16b",)], writes=[("p", 5)])
                S("dve", lambda e: e.tensor_copy(out=ev[0][:], in_=p[5][:]),
                  reads=[("p", 5)], writes=[("ev", 0)])
                S("dve", (lambda e, t=ts: e.tensor_mul(
                    mbuf[:, t * 512:(t + 1) * 512], p[t][:], ev[0][:])),
                    reads=[("p", t if False else ts), ("ev", 0)], writes=[("mbuf",)])

        def emit_base(res_res, res_ap, res_col, dst_key, dst_ap, dst_col, is_out):
            for oc in range(32):
                wbP, wbres = stream_chunk(
                    (lambda h, o=oc: wbB[o, :, h * 2112:(h + 1) * 2112]), oc, split_B=True)
                P = oc % 2
                base = 4 * P

                def mmloop(e, t=wbP, bb=base):
                    for dc in range(32):
                        lh = t[:, dc * 128:(dc + 1) * 128]
                        for ts in range(4):
                            e.matmul(p[bb + ts][:], lh, xnslice(ts, dc),
                                     start=(dc == 0), stop=False)
                    lhB = t[:, 4096:4224]
                    for ts in range(4):
                        i = e.matmul(p[bb + ts][:], lhB, mbuf[:, ts * 512:(ts + 1) * 512],
                                     start=False, stop=True)
                    return i
                S("te", mmloop, reads=wbres + [("mbuf",)] + XNT,
                  writes=[("p", base), ("p", base + 1), ("p", base + 2), ("p", base + 3)])
                for ts in range(4):
                    eI, eO = ev[ts % 2], ev[2 + ts % 2]
                    ra = res_ap[oc * 128:(oc + 1) * 128,
                                res_col + ts * 512: res_col + (ts + 1) * 512]
                    da = dst_ap[oc * 128:(oc + 1) * 128,
                                dst_col + ts * 512: dst_col + (ts + 1) * 512]
                    S("sync", (lambda e, b=eI, a=ra: e.dma_start(out=b[:], in_=a)),
                      reads=list(res_res), writes=[("ev", ts % 2)], dma=True, chan=f"cev{ts % 2}")
                    S("dve", (lambda e, o=oc, bi=eI, bo=eO, bk=base + ts: e.scalar_tensor_tensor(
                        out=bo[:], in0=p[bk][:], scalar=bbh[:, o:o + 1], in1=bi[:],
                        op0=ALU.add, op1=ALU.add)),
                        reads=[("p", base + ts), ("ev", ts % 2), ("bbh",)],
                        writes=[("ev", 2 + ts % 2)])
                    S("sync", (lambda e, b=eO, a=da: e.dma_start(out=a, in_=b[:])),
                      reads=[("ev", 2 + ts % 2)], writes=[(dst_key, ts % 2)], dma=True,
                      force_inc=is_out, chan=f"cst{ts % 2}")

        nrm = {"q": [None], "k": [None]}

        def emit_qkv(grp, do_q, n_k_calls, n_q_calls):
            col = grp * 2048
            chunks = [("k", i) for i in range(8)] + [("v", i) for i in range(8)]
            if do_q:
                chunks += [("q", i) for i in range(32)]
            for ci, (kind, oc) in enumerate(chunks):
                src = {"k": wk4, "v": wv4, "q": wq4}[kind]
                wbP, wbres = stream_chunk(
                    (lambda h, s=src, o=oc: s[o, :, h * 2048:(h + 1) * 2048]), ci,
                    scale=(QSCALE if kind == "q" else None))

                def mmloop(e, t=wbP):
                    for dc in range(32):
                        lh = t[:, dc * 128:(dc + 1) * 128]
                        for ts in range(4):
                            i = e.matmul(p[ts][:], lh, xnslice(ts, dc),
                                         start=(dc == 0), stop=(dc == 31))
                    return i
                S("te", mmloop, reads=wbres + XNT,
                  writes=[("p", 0), ("p", 1), ("p", 2), ("p", 3)])
                for ts in range(4):
                    if kind in ("k", "q"):
                        S("act", (lambda e, t=ts: e.activation(out=pt[0][:], in_=p[t][:], func=AF.Copy)),
                          reads=[("p", ts)], writes=[("pt", 0)])
                        if kind == "k":
                            dsta = ks[oc * 128:(oc + 1) * 128, col + ts * 512: col + (ts + 1) * 512]
                            dres = ("ks",)
                        else:
                            dsta = qs[oc * 128:(oc + 1) * 128, ts * 512:(ts + 1) * 512]
                            dres = ("qs",)
                        S("sync", (lambda e, a=dsta: e.dma_start(out=a, in_=pt[0][:])),
                          reads=[("pt", 0)], writes=[dres], dma=True, chan="ckq")
                        S("act", lambda e: e.activation(out=pt[1][:], in_=pt[0][:], func=AF.Square),
                          reads=[("pt", 0)], writes=[("pt", 1)])
                        S("te", lambda e: e.matmul(p[5][0:1, :], onesc[:], pt[1][:],
                                                   start=True, stop=True),
                          reads=[("pt", 1), ("onesc",)], writes=[("p", 5)])
                        S("dve", lambda e: e.reduce_max(gsm[0:1, 15:16], p[5][0:1, :], axis=AX.X),
                          reads=[("p", 5)], writes=[("ntmp",)])
                        cols = (10, 12) if kind == "q" else (11, 13)
                        prev = nrm[kind][0]
                        if prev is None:
                            cur = cols[0]
                            S("dve", (lambda e, cc=cur: e.tensor_copy(
                                out=gsm[0:1, cc:cc + 1], in_=gsm[0:1, 15:16])),
                                reads=[("ntmp",)], writes=[("nrm", kind, cur)])
                        else:
                            cur = cols[0] if prev == cols[1] else cols[1]
                            S("dve", (lambda e, cc=cur, pv=prev: e.tensor_max(
                                gsm[0:1, cc:cc + 1], gsm[0:1, pv:pv + 1], gsm[0:1, 15:16])),
                                reads=[("ntmp",), ("nrm", kind, prev)],
                                writes=[("nrm", kind, cur)])
                        nrm[kind][0] = cur
                    else:
                        evv = ev[2 + ts % 2]
                        S("act", (lambda e, t=ts, o=evv: e.activation(out=o[:], in_=p[t][:], func=AF.Copy)),
                          reads=[("p", ts)], writes=[("ev", 2 + ts % 2)])
                        for q in range(2):
                            bank = (4, 7)[q]

                            def trf(e, qq=q, b=bank, srcb=evv):
                                e.transpose(p[b][0:128, 0:128],
                                            srcb[:, (2 * qq) * 128:(2 * qq + 1) * 128], iden[:])
                                return e.transpose(p[b][0:128, 128:256],
                                                   srcb[:, (2 * qq + 1) * 128:(2 * qq + 2) * 128], iden[:])
                            S("te", trf, reads=[("ev", 2 + ts % 2), ("iden",)], writes=[("p", bank)])
                            S("dve", (lambda e, qq=q, b=bank: e.tensor_copy(
                                out=pt[2][:, (2 * qq) * 128:(2 * qq + 2) * 128],
                                in_=p[b][0:128, 0:256])),
                                reads=[("p", bank)], writes=[("pt", 2)])
                        b0 = col // 128 + ts * 4
                        dsta = vs3[:, b0:b0 + 4, oc * 128:(oc + 1) * 128]
                        S("sync", (lambda e, a=dsta: e.dma_start(
                            out=a, in_=pt[2][:].rearrange("pp (q dd) -> pp q dd", dd=128))),
                            reads=[("pt", 2)], writes=[("vs",)], dma=True, chan="cvs")

        def emit_norm_final():
            qc, kc = nrm["q"][0], nrm["k"][0]
            S("dve", (lambda e, a=qc, b=kc: e.tensor_mul(
                gsm[0:1, 14:15], gsm[0:1, a:a + 1], gsm[0:1, b:b + 1])),
                reads=[("nrm", "q", qc), ("nrm", "k", kc)], writes=[("nm",)])
            S("act", lambda e: e.activation(out=gsm[0:1, 15:16], in_=gsm[0:1, 14:15], func=AF.Sqrt),
              reads=[("nm",)], writes=[("nm2",)])
            S("dve", lambda e: e.tensor_scalar_mul(gsm[0:1, 14:15], gsm[0:1, 15:16], -1.0),
              reads=[("nm2",)], writes=[("nm3",)])
            S("te", lambda e: e.matmul(p[4][:, 0:1], onesr[:], gsm[0:1, 14:15], start=True, stop=True),
              reads=[("nm3",), ("onesr",)], writes=[("p", 4)])
            S("dve", lambda e: e.tensor_copy(out=negM[:], in_=p[4][:, 0:1]),
              reads=[("p", 4)], writes=[("negM",)])

        def emit_attn():
            for h in range(2):
                src = maskm[h * 4:(h + 1) * 4].rearrange("j pp q -> pp j q")
                wfH = wf[h]
                S("sync", (lambda e, a=src, w=wfH: e.dma_start(
                    out=w[:, 0:2048].rearrange("pp (j q) -> pp j q", q=512), in_=a)),
                    writes=[("wf", h)], dma=True, chan=f"cwf{h}")
                S("pool", (lambda e, w=wfH, hh=h: e.tensor_copy(
                    out=mbuf[:, hh * 2048:(hh + 1) * 2048], in_=w[:, 0:2048])),
                    reads=[("wf", h)], writes=[("mbuf",)])
            for kh in range(8):
                S("sync", (lambda e, k=kh: e.dma_start(
                    out=wb[0][:, 0:4096], in_=ks[k * 128:(k + 1) * 128, :])),
                    reads=[("ks",)], writes=[("wbh", 0, 0), ("wbh", 0, 1)], dma=True, chan="cwb0")
                S("sync", (lambda e, k=kh: e.dma_start(
                    out=wb[1][:, 0:4096].rearrange("pp (blk dd) -> pp blk dd", dd=128),
                    in_=vs3[:, :, k * 128:(k + 1) * 128])),
                    reads=[("vs",)], writes=[("wbh", 1, 0), ("wbh", 1, 1)], dma=True, chan="cwb1")
                for gr in range(4):
                    hq = kh * 4 + gr
                    S("sync", (lambda e, h=hq: e.dma_start(
                        out=qg[:], in_=qs[h * 128:(h + 1) * 128, :])),
                        reads=[("qs",)], writes=[("qg",)], dma=True, chan="cqg")
                    for qt in range(4):
                        blocks = []
                        for reg in range(2):
                            for m in range(qt + 1):
                                for jj in range(4):
                                    cb = (reg * 16 + m * 4 + jj) * 128
                                    mslot = (reg * 4 + jj) if m == qt else None
                                    blocks.append((cb, mslot))
                        nb = len(blocks)
                        lav = []

                        def emit_lav(idx):
                            cb, mslot, par, buf = lav[idx]

                            def f(e, c=cb, bu=buf, s0=(idx == 0), s1=(idx == nb - 1)):
                                e.matmul(p[3][0:1, :], onesc[:], bu[:], start=s0, stop=s1)
                                return e.matmul(p[2][:], wb[1][:, c:c + 128], bu[:], start=s0, stop=s1)
                            S("te", f,
                              reads=[("ptt", par) if mslot is None else ("ptm", par),
                                     ("wbh", 1, 0), ("wbh", 1, 1), ("onesc",)],
                              writes=[("p", 2), ("p", 3)])

                        for idx, (cb, mslot) in enumerate(blocks):
                            par = idx % 2
                            S("te", (lambda e, c=cb, pr=par, t=qt: e.matmul(
                                p[pr][:], wb[0][:, c:c + 128], qg[:, t * 512:(t + 1) * 512],
                                start=True, stop=True)),
                                reads=[("wbh", 0, 0), ("wbh", 0, 1), ("qg",)],
                                writes=[("p", par)])
                            S("act", (lambda e, pr=par: e.activation(
                                out=pt[pr][:], in_=p[pr][:], func=AF.Exp, bias=negM[:])),
                                reads=[("p", par), ("negM",)], writes=[("ptt", par)])
                            if mslot is not None:
                                S("pool", (lambda e, pr=par, ms=mslot: e.tensor_mul(
                                    pt[2 + pr][:], pt[pr][:], mbuf[:, ms * 512:(ms + 1) * 512])),
                                    reads=[("ptt", par), ("mbuf",)], writes=[("ptm", par)])
                            lav.append((cb, mslot, par, pt[2 + par] if mslot is not None else pt[par]))
                            if idx >= 1:
                                emit_lav(idx - 1)
                        emit_lav(nb - 1)
                        S("dve", lambda e: e.reciprocal(st(0), p[3][0:1, :]),
                          reads=[("p", 3)], writes=[("st", 0)])
                        S("te", lambda e: e.matmul(p[4][:], onesr[:], st(0), start=True, stop=True),
                          reads=[("st", 0), ("onesr",)], writes=[("p", 4)])
                        S("dve", lambda e: e.tensor_copy(out=ev[2][:], in_=p[4][:]),
                          reads=[("p", 4)], writes=[("ev", 2)])
                        S("dve", (lambda e, h=hq, t=qt: e.tensor_mul(
                            xnr[:, h * 2048 + t * 512: h * 2048 + (t + 1) * 512],
                            p[2][:], ev[2][:])),
                            reads=[("p", 2), ("ev", 2)], writes=[("xnrt", qt)])
            for oc in range(32):
                wbP, wbres = stream_chunk(
                    (lambda h, o=oc: wo4[o, :, h * 2048:(h + 1) * 2048]), oc)
                P = oc % 2
                base = 4 * P

                def mmloop(e, t=wbP, bb=base):
                    for fc in range(32):
                        lh = t[:, fc * 128:(fc + 1) * 128]
                        for ts in range(4):
                            i = e.matmul(p[bb + ts][:], lh,
                                         xnr[:, fc * 2048 + ts * 512: fc * 2048 + (ts + 1) * 512],
                                         start=(fc == 0), stop=(fc == 31))
                    return i
                S("te", mmloop, reads=wbres + XNT,
                  writes=[("p", base), ("p", base + 1), ("p", base + 2), ("p", base + 3)])
                for ts in range(4):
                    eI, eO = ev[ts % 2], ev[2 + ts % 2]
                    S("sync", (lambda e, o=oc, t=ts, b=eI: e.dma_start(
                        out=b[:], in_=x1s[o * 128:(o + 1) * 128, t * 512:(t + 1) * 512])),
                        reads=[("x1s", 0), ("x1s", 1)], writes=[("ev", ts % 2)], dma=True, chan=f"cev{ts % 2}")
                    S("dve", (lambda e, bk=base + ts, bi=eI, bo=eO: e.tensor_add(
                        bo[:], p[bk][:], bi[:])),
                        reads=[("p", base + ts), ("ev", ts % 2)], writes=[("ev", 2 + ts % 2)])
                    S("sync", (lambda e, o=oc, t=ts, b=eO: e.dma_start(
                        out=x2s[o * 128:(o + 1) * 128, t * 512:(t + 1) * 512], in_=b[:])),
                        reads=[("ev", 2 + ts % 2)], writes=[("x2s", ts % 2)], dma=True, chan=f"cst{ts % 2}")

        X1S = [("x1s", 0), ("x1s", 1)]
        X2S = [("x2s", 0), ("x2s", 1)]
        if "f1a" in phases:
            emit_pre((), x, 0)
            emit_gating()
            emit_base((), x, 0, "x1s", x1s, 0, False)
        if "f1b" in phases:
            emit_pre((), x, 2048)
            emit_gating()
            emit_base((), x, 2048, "x1s", x1s, 2048, False)
        if "qa" in phases:
            emit_pre(X1S, x1s, 0)
            emit_qkv(0, True, 2, 1)
        if "qb" in phases:
            emit_pre(X1S, x1s, 2048)
            emit_qkv(1, False, 2, 1)
        if "at" in phases:
            emit_norm_final()
            emit_attn()
        if "f3" in phases:
            emit_pre(X2S, x2s, 0)
            emit_gating()
            emit_base(X2S, x2s, 0, "outr", out, 0, True)

        if dbg is not None:
            srcmap = {"x1s": (x1s, [("x1s", 0), ("x1s", 1)]),
                      "x2s": (x2s, [("x2s", 0), ("x2s", 1)]),
                      "qs": (qs, [("qs",)]), "ks": (ks, [("ks",)]), "vs": (vs, [("vs",)])}
            sap, rr = srcmap[dbg]
            nrow = sap.shape[0]
            for i in range(0, nrow, 512):
                S("sync", (lambda e, j=i: e.dma_start(out=dbg_out[j:j + 512, :],
                                                      in_=sap[j:j + 512, :])),
                  reads=rr, writes=[("dbgw",)], dma=True, force_inc=True, chan="cdbg")

        fin = []
        if "f3" in phases:
            fin += [("outr", 0), ("outr", 1)]
        if dbg is not None:
            fin.append(("dbgw",))
        if not fin:
            fin = [("ks",)]
        S("sync", lambda e: e.nop(), reads=fin)

        sch.finalize(nc, blk, ctx)
    return nc


def _chunkmajor(W):
    O, Dd = W.shape
    return np.ascontiguousarray(
        W.reshape(O // 128, 128, Dd // 128, 128).transpose(0, 3, 2, 1).reshape(O // 128, 128, Dd))


def _own_tokens(par):
    return np.concatenate([np.arange((8 * t + o) * 128, (8 * t + o) * 128 + 128)
                           for t in range(4) for o in OFFS[par]])


def _masks(par):
    m = np.zeros((8, 128, 512), np.float32)
    tri = (np.arange(128)[:, None] <= np.arange(128)[None, :]).astype(np.float32)
    for r, offs in enumerate((OFFS[par], OFFS[1 - par])):
        for jj, j in enumerate(offs):
            for i, oi in enumerate(OFFS[par]):
                if j < oi:
                    m[r * 4 + jj][:, i * 128:(i + 1) * 128] = 1.0
                elif j == oi:
                    m[r * 4 + jj][:, i * 128:(i + 1) * 128] = tri
    return m


def prep_inputs(inputs):
    x = np.asarray(inputs["L0_attn_x"], np.float32)
    Wq = np.asarray(inputs["L0_attn_Wq"], np.float32)
    Wk = np.asarray(inputs["L0_attn_Wk"], np.float32)
    Wv = np.asarray(inputs["L0_attn_Wv"], np.float32)
    Wo = np.asarray(inputs["L0_attn_Wo"], np.float32)
    Wb = np.asarray(inputs["L0_ffn_W_base"], np.float32)
    bb = np.asarray(inputs["L0_ffn_b_base"], np.float32)
    Wg = np.asarray(inputs["L0_ffn_W_gate"], np.float32)
    Wt = np.asarray(inputs["L0_ffn_W_thr"], np.float32)
    bt = np.asarray(inputs["L0_ffn_b_thr"], np.float32)
    Aw = np.asarray(inputs["L0_ffn_A_w"], np.float32)
    Bw = np.asarray(inputs["L0_ffn_B_w"], np.float32)

    wb4 = _chunkmajor(Wb)
    Bsb = np.ascontiguousarray(Bw.transpose(1, 0, 2).reshape(4096, 128).T)
    wbB = np.ascontiguousarray(np.concatenate(
        [wb4, Bsb.reshape(128, 32, 128).transpose(1, 0, 2)], axis=2))
    Af = Aw.reshape(128, 4096)
    awm = np.ascontiguousarray(Af.reshape(128, 32, 128).transpose(2, 1, 0).reshape(128, 4096))
    gtm = np.zeros((4096, 40), np.float32)
    gtm[:, 0:8] = Wg.T
    gtm[:, 32] = Wt
    shared = dict(
        wbB=wbB, wq4=_chunkmajor(Wq), wk4=_chunkmajor(Wk), wv4=_chunkmajor(Wv),
        wo4=_chunkmajor(Wo), gtm=gtm, awm=awm,
        bbm=np.ascontiguousarray(bb.reshape(32, 128).T),
        bthr=np.ascontiguousarray(bt.reshape(1, 1)),
        idenm=np.eye(128, dtype=np.float32),
        onesrm=np.ones((1, 128), np.float32),
        onescm=np.ones((128, 1), np.float32),
        e16m=np.kron(np.eye(8, dtype=np.float32), np.ones((1, 16), np.float32)))
    pmask = [_masks(0), _masks(1)]
    in_maps = []
    for c in range(8):
        b, par = c // 2, c % 2
        loc = np.concatenate([_own_tokens(par), _own_tokens(1 - par)])
        d = dict(shared)
        d["x"] = np.ascontiguousarray(x[b][loc].T)
        d["maskm"] = pmask[par]
        in_maps.append(d)
    return in_maps


def kernel(**inputs):
    in_maps = prep_inputs(inputs)
    nc = build()
    res = run_bass_kernel_spmd(nc, in_maps, core_ids=list(range(8)))
    out = np.empty((4, 4096, 4096), np.float32)
    for c in range(8):
        b, par = c // 2, c % 2
        out[b][_own_tokens(par)] = np.ascontiguousarray(res.results[c]["out"]).T
    return out


# version 02
# speedup vs baseline: 1.4065x; 1.4065x over previous
import contextlib
import math
import numpy as np
import ml_dtypes
import concourse.bass as bass
import concourse.mybir as mybir
from concourse.bass_utils import run_bass_kernel_spmd

F32 = mybir.dt.float32
BF16 = mybir.dt.bfloat16
AF = mybir.ActivationFunctionType
ALU = mybir.AluOpType
AX = mybir.AxisListType

OFFS = ((0, 3, 4, 7), (1, 2, 5, 6))
D = 4096
NOWN = 2048
NLOC = 4096
QSCALE = 1.0 / math.sqrt(128.0)
ENGS = ("te", "act", "dve", "pool", "sync")


class Sched:

    def __init__(self):
        self.events = []
        self.lastw = {}
        self.readers = {}
        self.chans = set()

    def ev(self, eng, fn, reads=(), writes=(), dma=False, force_inc=False, chan=None):
        deps = set()
        for r in reads:
            w = self.lastw.get(r)
            if w is not None:
                deps.add(w)
        for w_ in writes:
            lw = self.lastw.get(w_)
            if lw is not None:
                deps.add(lw)
            for rd in self.readers.get(w_, ()):
                deps.add(rd)
        if dma:
            assert chan is not None
            self.chans.add(chan)
        key = chan if dma else eng
        i = len(self.events)
        self.events.append([eng, fn, deps, dma, force_inc, key])
        for r in reads:
            self.readers.setdefault(r, set()).add(i)
        for w_ in writes:
            self.lastw[w_] = i
            self.readers[w_] = set()
        return i

    def finalize(self, nc, blk, ctx):
        n = len(self.events)
        keys = list(ENGS) + sorted(self.chans)
        sems = {k: ctx.enter_context(nc.semaphore(f"s{j}")) for j, k in enumerate(keys)}
        waits = []
        needed = set()
        for i, (eng, fn, deps, dma, forced, key) in enumerate(self.events):
            bykey = {}
            for d in deps:
                dk = self.events[d][5]
                if dk == "te" and key == "te":
                    continue
                v = bykey.get(dk)
                if v is None or d > v:
                    bykey[dk] = d
            waits.append(bykey)
            for d in bykey.values():
                needed.add(d)
            if forced:
                needed.add(i)
        val_after = [0] * n
        cum = {k: 0 for k in keys}
        for i, (eng, fn, deps, dma, forced, key) in enumerate(self.events):
            if i in needed:
                cum[key] += 16 if dma else 1
            val_after[i] = cum[key]
        per_eng = {e: [] for e in ENGS}
        for i, e in enumerate(self.events):
            per_eng[e[0]].append(i)
        events = self.events

        def mk(e):
            idxs = per_eng[e]

            def emit(engobj):
                water = {x: 0 for x in keys}
                for i in idxs:
                    _, fn, _, dma, _, key = events[i]
                    for dk, d in sorted(waits[i].items(), key=lambda kv: str(kv[0])):
                        v = val_after[d]
                        if v > water[dk]:
                            engobj.wait_ge(sems[dk], v)
                            water[dk] = v
                    inst = fn(engobj)
                    if i in needed:
                        inst.then_inc(sems[key], 16 if dma else 1)
            return emit

        blk.tensor(mk("te"))
        blk.scalar(mk("act"))
        blk.vector(mk("dve"))
        blk.gpsimd(mk("pool"))
        blk.sync(mk("sync"))


def build(phases=("f1a", "f1b", "qa", "qb", "at", "f3"), dbg=None):
    nc = bass.Bass("TRN2", target_bir_lowering=False)

    x = nc.dram_tensor("x", [D, NLOC], F32, kind="ExternalInput").ap()
    wbB = nc.dram_tensor("wbB", [32, 128, 4224], F32, kind="ExternalInput").ap()
    wq4 = nc.dram_tensor("wq4", [32, 128, 4096], F32, kind="ExternalInput").ap()
    wk4 = nc.dram_tensor("wk4", [8, 128, 4096], F32, kind="ExternalInput").ap()
    wv4 = nc.dram_tensor("wv4", [8, 128, 4096], F32, kind="ExternalInput").ap()
    wo4 = nc.dram_tensor("wo4", [32, 128, 4096], F32, kind="ExternalInput").ap()
    gtm = nc.dram_tensor("gtm", [4096, 40], F32, kind="ExternalInput").ap()
    awm = nc.dram_tensor("awm", [128, 4096], F32, kind="ExternalInput").ap()
    bbm = nc.dram_tensor("bbm", [128, 32], F32, kind="ExternalInput").ap()
    bthr = nc.dram_tensor("bthr", [1, 1], F32, kind="ExternalInput").ap()
    maskm = nc.dram_tensor("maskm", [8, 128, 512], F32, kind="ExternalInput").ap()
    idenm = nc.dram_tensor("idenm", [128, 128], F32, kind="ExternalInput").ap()
    onesrm = nc.dram_tensor("onesrm", [1, 128], F32, kind="ExternalInput").ap()
    onescm = nc.dram_tensor("onescm", [128, 1], F32, kind="ExternalInput").ap()
    e16m = nc.dram_tensor("e16m", [8, 128], F32, kind="ExternalInput").ap()

    out = nc.dram_tensor("out", [D, NOWN], F32, kind="ExternalOutput").ap()

    x1s = nc.dram_tensor("x1s", [D, NLOC], F32, kind="Internal").ap()
    x2s = nc.dram_tensor("x2s", [D, NOWN], F32, kind="Internal").ap()
    qs = nc.dram_tensor("qs", [D, NOWN], BF16, kind="Internal").ap()
    ks = nc.dram_tensor("ks", [1024, NLOC], BF16, kind="Internal").ap()
    vs = nc.dram_tensor("vs", [NLOC, 1024], BF16, kind="Internal").ap()
    vs3 = vs.rearrange("(blk pp) c -> pp blk c", pp=128)

    dbg_out = None
    if dbg is not None:
        shp, dt = {"x1s": ([D, NLOC], F32), "x2s": ([D, NOWN], F32),
                   "qs": ([D, NOWN], BF16), "ks": ([1024, NLOC], BF16),
                   "vs": ([NLOC, 1024], BF16)}[dbg]
        dbg_out = nc.dram_tensor("dbg", shp, dt, kind="ExternalOutput").ap()

    with contextlib.ExitStack() as ctx:
        E = ctx.enter_context
        sb = nc.sbuf_tensor
        xnr = E(sb("xnr", [128, 65536], BF16))
        wf = [E(sb(f"wf{i}", [128, 2112], F32)) for i in range(2)]
        wb = [E(sb(f"wb{i}", [128, 4224], BF16)) for i in range(2)]
        qg = E(sb("qg", [128, 2048], BF16))
        mbuf = E(sb("mbuf", [128, 4096], BF16))
        ev = [E(sb(f"ev{i}", [128, 512], F32)) for i in range(4)]
        pt = [E(sb(f"pt{i}", [128, 512], BF16)) for i in range(4)]
        stats = E(sb("stats", [1, 3072], F32))
        gtf = E(sb("gtf", [40, 512], F32))
        sgt = E(sb("sgt", [128, 40], F32))
        gsm = E(sb("gsm", [128, 16], F32))
        e8f = E(sb("e8f", [128, 8], F32))
        e8b = E(sb("e8b", [128, 8], F32))
        w8f = E(sb("w8f", [128, 8], F32))
        wnT = E(sb("wnT", [8, 2048], BF16))
        iden = E(sb("iden", [128, 128], F32))
        onesr = E(sb("onesr", [1, 128], F32))
        onesc32 = E(sb("onesc32", [128, 1], F32))
        onesc = E(sb("onesc", [128, 1], BF16))
        e16b = E(sb("e16b", [8, 128], BF16))
        e16f = E(sb("e16f", [8, 128], F32))
        bbh = E(sb("bbh", [128, 32], F32))
        bthrb = E(sb("bthrb", [128, 1], F32))
        negM = E(sb("negM", [128, 1], F32))
        p = [E(nc.psum_tensor(f"p{i}", [128, 512], F32)) for i in range(8)]
        blk = E(nc.Block())

        sch = Sched()
        S = sch.ev

        def st(i):
            return stats[0:1, i * 512:(i + 1) * 512]

        S("sync", lambda e: e.dma_start(out=iden[:], in_=idenm), writes=[("iden",)], dma=True, chan="c0")
        S("sync", lambda e: e.dma_start(out=onesr[:], in_=onesrm), writes=[("onesr",)], dma=True, chan="c1")
        S("sync", lambda e: e.dma_start(out=onesc32[:], in_=onescm), writes=[("onesc32",)], dma=True, chan="c2")
        S("sync", lambda e: e.dma_start(out=e16f[:], in_=e16m), writes=[("e16f",)], dma=True, chan="c3")
        S("sync", lambda e: e.dma_start(out=ev[1][:, 0:32], in_=bbm), writes=[("ev", 1)], dma=True, chan="c4")
        S("sync", lambda e: e.dma_start(out=gsm[0:1, 0:1], in_=bthr), writes=[("g", 0)], dma=True, chan="c5")
        S("pool", lambda e: e.tensor_copy(out=onesc[:], in_=onesc32[:]),
          reads=[("onesc32",)], writes=[("onesc",)])
        S("pool", lambda e: e.tensor_copy(out=e16b[:], in_=e16f[:]),
          reads=[("e16f",)], writes=[("e16b",)])
        S("pool", lambda e: e.tensor_scalar_mul(bbh[:], ev[1][:, 0:32], 0.5),
          reads=[("ev", 1)], writes=[("bbh",)])
        S("te", lambda e: e.matmul(p[7][:, 0:1], onesr[:], gsm[0:1, 0:1], start=True, stop=True),
          reads=[("g", 0), ("onesr",)], writes=[("p", 7)])
        S("dve", lambda e: e.tensor_copy(out=bthrb[:], in_=p[7][:, 0:1]),
          reads=[("p", 7)], writes=[("bthrb",)])

        def xnslice(ts, dc):
            c = dc * 2048 + ts * 512
            return xnr[:, c:c + 512]

        XNT = [("xnrt", t) for t in range(4)]

        def emit_pre(src_res, src_ap, col_base):
            src3 = src_ap.rearrange("(dc pp) t -> pp dc t", pp=128)
            xnr3 = xnr[:].rearrange("pp (dc t) -> pp dc t", t=2048)
            for s in range(32):
                ts = s // 8
                c0 = col_base + s * 64
                wfP = wf[s % 2]
                w3o = wfP[:, 0:2048].rearrange("pp (dc t) -> pp dc t", t=64)
                S("sync", (lambda e, o=w3o, c=c0: e.dma_start(out=o, in_=src3[:, :, c:c + 64])),
                  reads=list(src_res), writes=[("wf", s % 2)], dma=True, chan=f"cwf{s % 2}")
                cc = ts * 512 + (s % 8) * 64
                o3 = xnr3[:, :, cc:cc + 64]
                S("act", (lambda e, a=o3, b=w3o: e.activation(out=a, in_=b, func=AF.Copy)),
                  reads=[("wf", s % 2)], writes=[("xnrt", ts)])
            for ts in range(4):
                for dc in range(32):
                    sqb = pt[dc % 2]
                    S("act", (lambda e, o=sqb, i=xnslice(ts, dc): e.activation(
                        out=o[:], in_=i, func=AF.Square)),
                        reads=[("xnrt", ts)], writes=[("pt", dc % 2)])

                    def mm2(e, i=xnslice(ts, dc), q=sqb, s0=(dc == 0), s1=(dc == 31)):
                        e.matmul(p[0][0:1, :], onesc[:], i, start=s0, stop=s1)
                        return e.matmul(p[1][0:1, :], onesc[:], q[:], start=s0, stop=s1)
                    S("te", mm2, reads=[("xnrt", ts), ("pt", dc % 2), ("onesc",)],
                      writes=[("p", 0), ("p", 1)])
                S("dve", lambda e: e.tensor_scalar(st(0), p[0][0:1, :], 1.0 / 4096, None, ALU.mult),
                  reads=[("p", 0)], writes=[("st", 0)])
                S("dve", lambda e: e.tensor_scalar(st(1), p[1][0:1, :], 1.0 / 4096, None, ALU.mult),
                  reads=[("p", 1)], writes=[("st", 1)])
                S("dve", lambda e: e.tensor_mul(st(2), st(0), st(0)),
                  reads=[("st", 0)], writes=[("st", 2)])
                S("dve", lambda e: e.scalar_tensor_tensor(
                    out=st(3), in0=st(1), scalar=1e-5, in1=st(2),
                    op0=ALU.add, op1=ALU.subtract),
                  reads=[("st", 1), ("st", 2)], writes=[("st", 3)])
                S("act", lambda e: e.activation(out=st(4), in_=st(3), func=AF.Sqrt),
                  reads=[("st", 3)], writes=[("st", 4)])
                S("dve", lambda e: e.reciprocal(st(5), st(4)),
                  reads=[("st", 4)], writes=[("st", 5)])
                S("dve", lambda e: e.tensor_mul(st(2), st(0), st(5)),
                  reads=[("st", 0), ("st", 5)], writes=[("st", 2)])
                S("te", lambda e: e.matmul(p[2][:], onesr[:], st(5), start=True, stop=True),
                  reads=[("st", 5), ("onesr",)], writes=[("p", 2)])
                S("te", lambda e: e.matmul(p[3][:], onesr[:], st(2), start=True, stop=True),
                  reads=[("st", 2), ("onesr",)], writes=[("p", 3)])
                S("dve", lambda e: e.tensor_copy(out=ev[2][:], in_=p[2][:]),
                  reads=[("p", 2)], writes=[("ev", 2)])
                S("dve", lambda e: e.tensor_copy(out=ev[3][:], in_=p[3][:]),
                  reads=[("p", 3)], writes=[("ev", 3)])
                for dc in range(32):
                    evP = ev[dc % 2]
                    S("dve", (lambda e, o=evP, i=xnslice(ts, dc): e.tensor_mul(o[:], i, ev[2][:])),
                        reads=[("xnrt", ts), ("ev", 2)], writes=[("ev", dc % 2)])
                    S("pool", (lambda e, o=xnslice(ts, dc), i=evP: e.tensor_sub(o, i[:], ev[3][:])),
                        reads=[("ev", dc % 2), ("ev", 3)], writes=[("xnrt", ts)])

        def stream_chunk(src_getter, ci, scale=None, split_B=False):
            P = ci % 2
            wbP = wb[P]
            for h in range(2):
                wfH = wf[h]
                S("sync", (lambda e, a=src_getter(h), w=wfH: e.dma_start(out=w[:, 0:a.shape[1]], in_=a)),
                  writes=[("wf", h)], dma=True, chan=f"cwf{h}")
                if split_B and h == 1:
                    def cst(e, w=wfH, t=wbP):
                        e.activation(out=t[:, 2112:4096], in_=w[:, 0:1984], func=AF.Copy, scale=0.5)
                        return e.activation(out=t[:, 4096:4224], in_=w[:, 1984:2112], func=AF.Copy)
                    S("act", cst, reads=[("wf", h)], writes=[("wbh", P, h)])
                elif split_B and h == 0:
                    S("act", (lambda e, w=wfH, t=wbP: e.activation(
                        out=t[:, 0:2112], in_=w[:, 0:2112], func=AF.Copy, scale=0.5)),
                      reads=[("wf", h)], writes=[("wbh", P, h)])
                elif scale is not None:
                    S("act", (lambda e, w=wfH, t=wbP, hh=h, sc=scale: e.activation(
                        out=t[:, hh * 2048:(hh + 1) * 2048], in_=w[:, 0:2048], func=AF.Copy, scale=sc)),
                        reads=[("wf", h)], writes=[("wbh", P, h)])
                else:
                    S("act", (lambda e, w=wfH, t=wbP, hh=h: e.activation(
                        out=t[:, hh * 2048:(hh + 1) * 2048], in_=w[:, 0:2048], func=AF.Copy)),
                        reads=[("wf", h)], writes=[("wbh", P, h)])
            return wbP, [("wbh", P, 0), ("wbh", P, 1)]

        def emit_gating():
            gt3i = gtm.rearrange("(dc pp) c -> pp dc c", pp=128)
            gt3o = wf[0][:, 0:1280].rearrange("pp (dc c) -> pp dc c", c=40)
            S("sync", lambda e: e.dma_start(out=gt3o, in_=gt3i), writes=[("wf", 0)], dma=True, chan="cwf0")
            S("act", lambda e: e.activation(out=wb[0][:, 0:1280], in_=wf[0][:, 0:1280], func=AF.Copy),
              reads=[("wf", 0)], writes=[("wbh", 0, 0), ("wbh", 0, 1)])

            def gtloop(e):
                for dc in range(32):
                    lh = wb[0][:, dc * 40:(dc + 1) * 40]
                    for ts in range(4):
                        i = e.matmul(p[ts][0:40, :], lh, xnslice(ts, dc),
                                     start=(dc == 0), stop=(dc == 31))
                return i
            S("te", gtloop, reads=[("wbh", 0, 0), ("wbh", 0, 1)] + XNT,
              writes=[("p", 0), ("p", 1), ("p", 2), ("p", 3)])
            S("sync", lambda e: e.dma_start(out=wf[1][:, 0:2048], in_=awm[:, 0:2048]),
              writes=[("wf", 1)], dma=True, chan="cwf1")
            S("act", lambda e: e.activation(out=wb[1][:, 0:2048], in_=wf[1][:, 0:2048], func=AF.Copy),
              reads=[("wf", 1)], writes=[("wbh", 1, 0)])
            S("sync", lambda e: e.dma_start(out=wf[1][:, 0:2048], in_=awm[:, 2048:4096]),
              writes=[("wf", 1)], dma=True, chan="cwf1")
            S("act", lambda e: e.activation(out=wb[1][:, 2048:4096], in_=wf[1][:, 0:2048], func=AF.Copy),
              reads=[("wf", 1)], writes=[("wbh", 1, 1)])
            for ts in range(4):
                S("dve", (lambda e, t=ts: e.tensor_copy(out=gtf[:], in_=p[t][0:40, :])),
                  reads=[("p", ts)], writes=[("gtf", ts)])
                for q in range(4):
                    bank = 6 + (q % 2)
                    S("te", (lambda e, t=ts, qq=q, b=bank: e.transpose(
                        p[b][0:128, 0:40], gtf[0:40, qq * 128:(qq + 1) * 128], iden[0:40, 0:40])),
                        reads=[("gtf", ts), ("iden",)], writes=[("p", bank)])
                    S("dve", (lambda e, b=bank: e.tensor_copy(out=sgt[:], in_=p[b][0:128, 0:40])),
                        reads=[("p", bank)], writes=[("sgt",)])
                    S("dve", lambda e: e.reduce_max(gsm[:, 0:1], sgt[:, 0:8], axis=AX.X),
                      reads=[("sgt",)], writes=[("g", 0)])
                    S("dve", lambda e: e.tensor_scalar_mul(gsm[:, 1:2], gsm[:, 0:1], -1.0),
                      reads=[("g", 0)], writes=[("g", 1)])
                    S("act", lambda e: e.activation(out=e8f[:], in_=sgt[:, 0:8], func=AF.Exp,
                                                    bias=gsm[:, 1:2]),
                      reads=[("sgt",), ("g", 1)], writes=[("e8f",)])
                    S("dve", lambda e: e.reduce_sum(gsm[:, 2:3], e8f[:], axis=AX.X),
                      reads=[("e8f",)], writes=[("g", 2)])
                    S("dve", lambda e: e.reciprocal(gsm[:, 3:4], gsm[:, 2:3]),
                      reads=[("g", 2)], writes=[("g", 3)])
                    S("act", lambda e: e.activation(out=gsm[:, 4:5], in_=sgt[:, 32:33],
                                                    func=AF.Sigmoid, bias=bthrb[:]),
                      reads=[("sgt",), ("bthrb",)], writes=[("g", 4)])
                    S("dve", lambda e: e.tensor_scalar_mul(gsm[:, 5:6], gsm[:, 4:5], 0.125),
                      reads=[("g", 4)], writes=[("g", 5)])
                    S("dve", lambda e: e.tensor_scalar_mul(e8b[:], e8f[:], gsm[:, 3:4]),
                      reads=[("e8f",), ("g", 3)], writes=[("e8b",)])
                    S("dve", lambda e: e.tensor_scalar(w8f[:], e8b[:], gsm[:, 5:6], None, ALU.subtract),
                      reads=[("e8b",), ("g", 5)], writes=[("w8f",)])
                    S("dve", lambda e: e.tensor_scalar_max(e8f[:], w8f[:], 0.0),
                      reads=[("w8f",)], writes=[("e8f",)])
                    S("dve", lambda e: e.reduce_sum(gsm[:, 6:7], e8f[:], axis=AX.X),
                      reads=[("e8f",)], writes=[("g", 6)])
                    S("dve", lambda e: e.tensor_single_scalar(gsm[:, 7:8], gsm[:, 6:7], 0.0, ALU.is_equal),
                      reads=[("g", 6)], writes=[("g", 7)])
                    S("dve", lambda e: e.tensor_add(gsm[:, 8:9], gsm[:, 6:7], gsm[:, 7:8]),
                      reads=[("g", 6), ("g", 7)], writes=[("g", 8)])
                    S("dve", lambda e: e.reciprocal(gsm[:, 9:10], gsm[:, 8:9]),
                      reads=[("g", 8)], writes=[("g", 9)])
                    S("dve", lambda e: e.tensor_scalar_mul(w8f[:], e8f[:], gsm[:, 9:10]),
                      reads=[("e8f",), ("g", 9)], writes=[("w8f",)])
                    S("te", (lambda e, b=bank: e.transpose(p[b][0:8, 0:128], w8f[:, 0:8], iden[:])),
                      reads=[("w8f",), ("iden",)], writes=[("p", bank)])
                    S("dve", (lambda e, t=ts, qq=q, b=bank: e.tensor_copy(
                        out=wnT[0:8, t * 512 + qq * 128: t * 512 + (qq + 1) * 128],
                        in_=p[b][0:8, 0:128])),
                        reads=[("p", bank)], writes=[("wnT", ts)])

            def hfloop(e):
                for dc in range(32):
                    lh = wb[1][:, dc * 128:(dc + 1) * 128]
                    for ts in range(4):
                        i = e.matmul(p[ts][:], lh, xnslice(ts, dc),
                                     start=(dc == 0), stop=(dc == 31))
                return i
            S("te", hfloop,
              reads=[("wbh", 1, 0), ("wbh", 1, 1)] + XNT + [("gtf", t) for t in range(4)],
              writes=[("p", 0), ("p", 1), ("p", 2), ("p", 3)])
            for ts in range(4):
                S("te", (lambda e, t=ts: e.matmul(p[5][:], e16b[:], wnT[0:8, t * 512:(t + 1) * 512],
                                                  start=True, stop=True)),
                  reads=[("wnT", ts), ("e16b",)], writes=[("p", 5)])
                S("dve", lambda e: e.tensor_copy(out=ev[0][:], in_=p[5][:]),
                  reads=[("p", 5)], writes=[("ev", 0)])
                S("dve", (lambda e, t=ts: e.tensor_mul(
                    mbuf[:, t * 512:(t + 1) * 512], p[t][:], ev[0][:])),
                    reads=[("p", t if False else ts), ("ev", 0)], writes=[("mbuf",)])

        def emit_base(res_res, res_ap, res_col, dst_key, dst_ap, dst_col, is_out):
            for oc in range(32):
                wbP, wbres = stream_chunk(
                    (lambda h, o=oc: wbB[o, :, h * 2112:(h + 1) * 2112]), oc, split_B=True)
                P = oc % 2
                base = 4 * P

                def mmloop(e, t=wbP, bb=base):
                    for dc in range(32):
                        lh = t[:, dc * 128:(dc + 1) * 128]
                        for ts in range(4):
                            e.matmul(p[bb + ts][:], lh, xnslice(ts, dc),
                                     start=(dc == 0), stop=False)
                    lhB = t[:, 4096:4224]
                    for ts in range(4):
                        i = e.matmul(p[bb + ts][:], lhB, mbuf[:, ts * 512:(ts + 1) * 512],
                                     start=False, stop=True)
                    return i
                S("te", mmloop, reads=wbres + [("mbuf",)] + XNT,
                  writes=[("p", base), ("p", base + 1), ("p", base + 2), ("p", base + 3)])
                for ts in range(4):
                    eI, eO = ev[ts % 2], ev[2 + ts % 2]
                    ra = res_ap[oc * 128:(oc + 1) * 128,
                                res_col + ts * 512: res_col + (ts + 1) * 512]
                    da = dst_ap[oc * 128:(oc + 1) * 128,
                                dst_col + ts * 512: dst_col + (ts + 1) * 512]
                    S("sync", (lambda e, b=eI, a=ra: e.dma_start(out=b[:], in_=a)),
                      reads=list(res_res), writes=[("ev", ts % 2)], dma=True, chan=f"cev{ts % 2}")
                    S("dve", (lambda e, o=oc, bi=eI, bo=eO, bk=base + ts: e.scalar_tensor_tensor(
                        out=bo[:], in0=p[bk][:], scalar=bbh[:, o:o + 1], in1=bi[:],
                        op0=ALU.add, op1=ALU.add)),
                        reads=[("p", base + ts), ("ev", ts % 2), ("bbh",)],
                        writes=[("ev", 2 + ts % 2)])
                    S("sync", (lambda e, b=eO, a=da: e.dma_start(out=a, in_=b[:])),
                      reads=[("ev", 2 + ts % 2)], writes=[(dst_key, ts % 2)], dma=True,
                      force_inc=is_out, chan=f"cst{ts % 2}")

        nrm = {"q": [None], "k": [None]}

        def emit_qkv(grp, do_q, n_k_calls, n_q_calls):
            col = grp * 2048
            chunks = [("k", i) for i in range(8)] + [("v", i) for i in range(8)]
            if do_q:
                chunks += [("q", i) for i in range(32)]
            for ci, (kind, oc) in enumerate(chunks):
                src = {"k": wk4, "v": wv4, "q": wq4}[kind]
                wbP, wbres = stream_chunk(
                    (lambda h, s=src, o=oc: s[o, :, h * 2048:(h + 1) * 2048]), ci,
                    scale=(QSCALE if kind == "q" else None))

                def mmloop(e, t=wbP):
                    for dc in range(32):
                        lh = t[:, dc * 128:(dc + 1) * 128]
                        for ts in range(4):
                            i = e.matmul(p[ts][:], lh, xnslice(ts, dc),
                                         start=(dc == 0), stop=(dc == 31))
                    return i
                S("te", mmloop, reads=wbres + XNT,
                  writes=[("p", 0), ("p", 1), ("p", 2), ("p", 3)])
                for ts in range(4):
                    if kind in ("k", "q"):
                        S("act", (lambda e, t=ts: e.activation(out=pt[0][:], in_=p[t][:], func=AF.Copy)),
                          reads=[("p", ts)], writes=[("pt", 0)])
                        if kind == "k":
                            dsta = ks[oc * 128:(oc + 1) * 128, col + ts * 512: col + (ts + 1) * 512]
                            dres = ("ks",)
                        else:
                            dsta = qs[oc * 128:(oc + 1) * 128, ts * 512:(ts + 1) * 512]
                            dres = ("qs",)
                        S("sync", (lambda e, a=dsta: e.dma_start(out=a, in_=pt[0][:])),
                          reads=[("pt", 0)], writes=[dres], dma=True, chan="ckq")
                        S("act", lambda e: e.activation(out=pt[1][:], in_=pt[0][:], func=AF.Square),
                          reads=[("pt", 0)], writes=[("pt", 1)])
                        S("te", lambda e: e.matmul(p[5][0:1, :], onesc[:], pt[1][:],
                                                   start=True, stop=True),
                          reads=[("pt", 1), ("onesc",)], writes=[("p", 5)])
                        S("dve", lambda e: e.reduce_max(gsm[0:1, 15:16], p[5][0:1, :], axis=AX.X),
                          reads=[("p", 5)], writes=[("ntmp",)])
                        cols = (10, 12) if kind == "q" else (11, 13)
                        prev = nrm[kind][0]
                        if prev is None:
                            cur = cols[0]
                            S("dve", (lambda e, cc=cur: e.tensor_copy(
                                out=gsm[0:1, cc:cc + 1], in_=gsm[0:1, 15:16])),
                                reads=[("ntmp",)], writes=[("nrm", kind, cur)])
                        else:
                            cur = cols[0] if prev == cols[1] else cols[1]
                            S("dve", (lambda e, cc=cur, pv=prev: e.tensor_max(
                                gsm[0:1, cc:cc + 1], gsm[0:1, pv:pv + 1], gsm[0:1, 15:16])),
                                reads=[("ntmp",), ("nrm", kind, prev)],
                                writes=[("nrm", kind, cur)])
                        nrm[kind][0] = cur
                    else:
                        evv = ev[2 + ts % 2]
                        S("act", (lambda e, t=ts, o=evv: e.activation(out=o[:], in_=p[t][:], func=AF.Copy)),
                          reads=[("p", ts)], writes=[("ev", 2 + ts % 2)])
                        for q in range(2):
                            bank = (4, 7)[q]

                            def trf(e, qq=q, b=bank, srcb=evv):
                                e.transpose(p[b][0:128, 0:128],
                                            srcb[:, (2 * qq) * 128:(2 * qq + 1) * 128], iden[:])
                                return e.transpose(p[b][0:128, 128:256],
                                                   srcb[:, (2 * qq + 1) * 128:(2 * qq + 2) * 128], iden[:])
                            S("te", trf, reads=[("ev", 2 + ts % 2), ("iden",)], writes=[("p", bank)])
                            S("dve", (lambda e, qq=q, b=bank: e.tensor_copy(
                                out=pt[2][:, (2 * qq) * 128:(2 * qq + 2) * 128],
                                in_=p[b][0:128, 0:256])),
                                reads=[("p", bank)], writes=[("pt", 2)])
                        b0 = col // 128 + ts * 4
                        dsta = vs3[:, b0:b0 + 4, oc * 128:(oc + 1) * 128]
                        S("sync", (lambda e, a=dsta: e.dma_start(
                            out=a, in_=pt[2][:].rearrange("pp (q dd) -> pp q dd", dd=128))),
                            reads=[("pt", 2)], writes=[("vs",)], dma=True, chan="cvs")

        def emit_norm_final():
            qc, kc = nrm["q"][0], nrm["k"][0]
            S("dve", (lambda e, a=qc, b=kc: e.tensor_mul(
                gsm[0:1, 14:15], gsm[0:1, a:a + 1], gsm[0:1, b:b + 1])),
                reads=[("nrm", "q", qc), ("nrm", "k", kc)], writes=[("nm",)])
            S("act", lambda e: e.activation(out=gsm[0:1, 15:16], in_=gsm[0:1, 14:15], func=AF.Sqrt),
              reads=[("nm",)], writes=[("nm2",)])
            S("dve", lambda e: e.tensor_scalar_mul(gsm[0:1, 14:15], gsm[0:1, 15:16], -1.0),
              reads=[("nm2",)], writes=[("nm3",)])
            S("te", lambda e: e.matmul(p[4][:, 0:1], onesr[:], gsm[0:1, 14:15], start=True, stop=True),
              reads=[("nm3",), ("onesr",)], writes=[("p", 4)])
            S("dve", lambda e: e.tensor_copy(out=negM[:], in_=p[4][:, 0:1]),
              reads=[("p", 4)], writes=[("negM",)])

        def emit_attn():
            for h in range(2):
                src = maskm[h * 4:(h + 1) * 4].rearrange("j pp q -> pp j q")
                wfH = wf[h]
                S("sync", (lambda e, a=src, w=wfH: e.dma_start(
                    out=w[:, 0:2048].rearrange("pp (j q) -> pp j q", q=512), in_=a)),
                    writes=[("wf", h)], dma=True, chan=f"cwf{h}")
                S("act", (lambda e, w=wfH, hh=h: e.activation(
                    out=mbuf[:, hh * 2048:(hh + 1) * 2048], in_=w[:, 0:2048], func=AF.Copy)),
                    reads=[("wf", h)], writes=[("mbuf",)])
            for kh in range(8):
                S("sync", (lambda e, k=kh: e.dma_start(
                    out=wb[0][:, 0:4096], in_=ks[k * 128:(k + 1) * 128, :])),
                    reads=[("ks",)], writes=[("wbh", 0, 0), ("wbh", 0, 1)], dma=True, chan="cwb0")
                S("sync", (lambda e, k=kh: e.dma_start(
                    out=wb[1][:, 0:4096].rearrange("pp (blk dd) -> pp blk dd", dd=128),
                    in_=vs3[:, :, k * 128:(k + 1) * 128])),
                    reads=[("vs",)], writes=[("wbh", 1, 0), ("wbh", 1, 1)], dma=True, chan="cwb1")
                for gr in range(4):
                    hq = kh * 4 + gr
                    S("sync", (lambda e, h=hq: e.dma_start(
                        out=qg[:], in_=qs[h * 128:(h + 1) * 128, :])),
                        reads=[("qs",)], writes=[("qg",)], dma=True, chan="cqg")
                    for qt in range(4):
                        blocks = []
                        for reg in range(2):
                            for m in range(qt + 1):
                                for jj in range(4):
                                    cb = (reg * 16 + m * 4 + jj) * 128
                                    mslot = (reg * 4 + jj) if m == qt else None
                                    blocks.append((cb, mslot))
                        nb = len(blocks)
                        lav = []

                        def emit_lav(idx):
                            cb, mslot, par, buf = lav[idx]

                            def f(e, c=cb, bu=buf, s0=(idx == 0), s1=(idx == nb - 1)):
                                e.matmul(p[3][0:1, :], onesc[:], bu[:], start=s0, stop=s1)
                                return e.matmul(p[2][:], wb[1][:, c:c + 128], bu[:], start=s0, stop=s1)
                            S("te", f,
                              reads=[("ptt", par) if mslot is None else ("ptm", par),
                                     ("wbh", 1, 0), ("wbh", 1, 1), ("onesc",)],
                              writes=[("p", 2), ("p", 3)])

                        for idx, (cb, mslot) in enumerate(blocks):
                            par = idx % 2
                            S("te", (lambda e, c=cb, pr=par, t=qt: e.matmul(
                                p[pr][:], wb[0][:, c:c + 128], qg[:, t * 512:(t + 1) * 512],
                                start=True, stop=True)),
                                reads=[("wbh", 0, 0), ("wbh", 0, 1), ("qg",)],
                                writes=[("p", par)])
                            S("act", (lambda e, pr=par: e.activation(
                                out=pt[pr][:], in_=p[pr][:], func=AF.Exp, bias=negM[:])),
                                reads=[("p", par), ("negM",)], writes=[("ptt", par)])
                            if mslot is not None:
                                S("pool", (lambda e, pr=par, ms=mslot: e.tensor_mul(
                                    pt[2 + pr][:], pt[pr][:], mbuf[:, ms * 512:(ms + 1) * 512])),
                                    reads=[("ptt", par), ("mbuf",)], writes=[("ptm", par)])
                            lav.append((cb, mslot, par, pt[2 + par] if mslot is not None else pt[par]))
                            if idx >= 1:
                                emit_lav(idx - 1)
                        emit_lav(nb - 1)
                        S("dve", lambda e: e.reciprocal(st(0), p[3][0:1, :]),
                          reads=[("p", 3)], writes=[("st", 0)])
                        S("te", lambda e: e.matmul(p[4][:], onesr[:], st(0), start=True, stop=True),
                          reads=[("st", 0), ("onesr",)], writes=[("p", 4)])
                        S("dve", lambda e: e.tensor_copy(out=ev[2][:], in_=p[4][:]),
                          reads=[("p", 4)], writes=[("ev", 2)])
                        S("dve", (lambda e, h=hq, t=qt: e.tensor_mul(
                            xnr[:, h * 2048 + t * 512: h * 2048 + (t + 1) * 512],
                            p[2][:], ev[2][:])),
                            reads=[("p", 2), ("ev", 2)], writes=[("xnrt", qt)])
            for oc in range(32):
                wbP, wbres = stream_chunk(
                    (lambda h, o=oc: wo4[o, :, h * 2048:(h + 1) * 2048]), oc)
                P = oc % 2
                base = 4 * P

                def mmloop(e, t=wbP, bb=base):
                    for fc in range(32):
                        lh = t[:, fc * 128:(fc + 1) * 128]
                        for ts in range(4):
                            i = e.matmul(p[bb + ts][:], lh,
                                         xnr[:, fc * 2048 + ts * 512: fc * 2048 + (ts + 1) * 512],
                                         start=(fc == 0), stop=(fc == 31))
                    return i
                S("te", mmloop, reads=wbres + XNT,
                  writes=[("p", base), ("p", base + 1), ("p", base + 2), ("p", base + 3)])
                for ts in range(4):
                    eI, eO = ev[ts % 2], ev[2 + ts % 2]
                    S("sync", (lambda e, o=oc, t=ts, b=eI: e.dma_start(
                        out=b[:], in_=x1s[o * 128:(o + 1) * 128, t * 512:(t + 1) * 512])),
                        reads=[("x1s", 0), ("x1s", 1)], writes=[("ev", ts % 2)], dma=True, chan=f"cev{ts % 2}")
                    S("dve", (lambda e, bk=base + ts, bi=eI, bo=eO: e.tensor_add(
                        bo[:], p[bk][:], bi[:])),
                        reads=[("p", base + ts), ("ev", ts % 2)], writes=[("ev", 2 + ts % 2)])
                    S("sync", (lambda e, o=oc, t=ts, b=eO: e.dma_start(
                        out=x2s[o * 128:(o + 1) * 128, t * 512:(t + 1) * 512], in_=b[:])),
                        reads=[("ev", 2 + ts % 2)], writes=[("x2s", ts % 2)], dma=True, chan=f"cst{ts % 2}")

        X1S = [("x1s", 0), ("x1s", 1)]
        X2S = [("x2s", 0), ("x2s", 1)]
        if "f1a" in phases:
            emit_pre((), x, 0)
            emit_gating()
            emit_base((), x, 0, "x1s", x1s, 0, False)
        if "f1b" in phases:
            emit_pre((), x, 2048)
            emit_gating()
            emit_base((), x, 2048, "x1s", x1s, 2048, False)
        if "qa" in phases:
            emit_pre(X1S, x1s, 0)
            emit_qkv(0, True, 2, 1)
        if "qb" in phases:
            emit_pre(X1S, x1s, 2048)
            emit_qkv(1, False, 2, 1)
        if "at" in phases:
            emit_norm_final()
            emit_attn()
        if "f3" in phases:
            emit_pre(X2S, x2s, 0)
            emit_gating()
            emit_base(X2S, x2s, 0, "outr", out, 0, True)

        if dbg is not None:
            srcmap = {"x1s": (x1s, [("x1s", 0), ("x1s", 1)]),
                      "x2s": (x2s, [("x2s", 0), ("x2s", 1)]),
                      "qs": (qs, [("qs",)]), "ks": (ks, [("ks",)]), "vs": (vs, [("vs",)])}
            sap, rr = srcmap[dbg]
            nrow = sap.shape[0]
            for i in range(0, nrow, 512):
                S("sync", (lambda e, j=i: e.dma_start(out=dbg_out[j:j + 512, :],
                                                      in_=sap[j:j + 512, :])),
                  reads=rr, writes=[("dbgw",)], dma=True, force_inc=True, chan="cdbg")

        fin = []
        if "f3" in phases:
            fin += [("outr", 0), ("outr", 1)]
        if dbg is not None:
            fin.append(("dbgw",))
        if not fin:
            fin = [("ks",)]
        S("sync", lambda e: e.nop(), reads=fin)

        sch.finalize(nc, blk, ctx)
    return nc


def _chunkmajor(W):
    O, Dd = W.shape
    return np.ascontiguousarray(
        W.reshape(O // 128, 128, Dd // 128, 128).transpose(0, 3, 2, 1).reshape(O // 128, 128, Dd))


def _own_tokens(par):
    return np.concatenate([np.arange((8 * t + o) * 128, (8 * t + o) * 128 + 128)
                           for t in range(4) for o in OFFS[par]])


def _masks(par):
    m = np.zeros((8, 128, 512), np.float32)
    tri = (np.arange(128)[:, None] <= np.arange(128)[None, :]).astype(np.float32)
    for r, offs in enumerate((OFFS[par], OFFS[1 - par])):
        for jj, j in enumerate(offs):
            for i, oi in enumerate(OFFS[par]):
                if j < oi:
                    m[r * 4 + jj][:, i * 128:(i + 1) * 128] = 1.0
                elif j == oi:
                    m[r * 4 + jj][:, i * 128:(i + 1) * 128] = tri
    return m


def prep_inputs(inputs):
    x = np.asarray(inputs["L0_attn_x"], np.float32)
    Wq = np.asarray(inputs["L0_attn_Wq"], np.float32)
    Wk = np.asarray(inputs["L0_attn_Wk"], np.float32)
    Wv = np.asarray(inputs["L0_attn_Wv"], np.float32)
    Wo = np.asarray(inputs["L0_attn_Wo"], np.float32)
    Wb = np.asarray(inputs["L0_ffn_W_base"], np.float32)
    bb = np.asarray(inputs["L0_ffn_b_base"], np.float32)
    Wg = np.asarray(inputs["L0_ffn_W_gate"], np.float32)
    Wt = np.asarray(inputs["L0_ffn_W_thr"], np.float32)
    bt = np.asarray(inputs["L0_ffn_b_thr"], np.float32)
    Aw = np.asarray(inputs["L0_ffn_A_w"], np.float32)
    Bw = np.asarray(inputs["L0_ffn_B_w"], np.float32)

    wb4 = _chunkmajor(Wb)
    Bsb = np.ascontiguousarray(Bw.transpose(1, 0, 2).reshape(4096, 128).T)
    wbB = np.ascontiguousarray(np.concatenate(
        [wb4, Bsb.reshape(128, 32, 128).transpose(1, 0, 2)], axis=2))
    Af = Aw.reshape(128, 4096)
    awm = np.ascontiguousarray(Af.reshape(128, 32, 128).transpose(2, 1, 0).reshape(128, 4096))
    gtm = np.zeros((4096, 40), np.float32)
    gtm[:, 0:8] = Wg.T
    gtm[:, 32] = Wt
    shared = dict(
        wbB=wbB, wq4=_chunkmajor(Wq), wk4=_chunkmajor(Wk), wv4=_chunkmajor(Wv),
        wo4=_chunkmajor(Wo), gtm=gtm, awm=awm,
        bbm=np.ascontiguousarray(bb.reshape(32, 128).T),
        bthr=np.ascontiguousarray(bt.reshape(1, 1)),
        idenm=np.eye(128, dtype=np.float32),
        onesrm=np.ones((1, 128), np.float32),
        onescm=np.ones((128, 1), np.float32),
        e16m=np.kron(np.eye(8, dtype=np.float32), np.ones((1, 16), np.float32)))
    pmask = [_masks(0), _masks(1)]
    in_maps = []
    for c in range(8):
        b, par = c // 2, c % 2
        loc = np.concatenate([_own_tokens(par), _own_tokens(1 - par)])
        d = dict(shared)
        d["x"] = np.ascontiguousarray(x[b][loc].T)
        d["maskm"] = pmask[par]
        in_maps.append(d)
    return in_maps


def kernel(**inputs):
    in_maps = prep_inputs(inputs)
    nc = build()
    res = run_bass_kernel_spmd(nc, in_maps, core_ids=list(range(8)))
    out = np.empty((4, 4096, 4096), np.float32)
    for c in range(8):
        b, par = c // 2, c % 2
        out[b][_own_tokens(par)] = np.ascontiguousarray(res.results[c]["out"]).T
    return out


# version 03 (final)
# speedup vs baseline: 1.5654x; 1.1130x over previous
import contextlib
import math
import numpy as np
import ml_dtypes
import concourse.bass as bass
import concourse.mybir as mybir
from concourse.bass_utils import run_bass_kernel_spmd

F32 = mybir.dt.float32
BF16 = mybir.dt.bfloat16
AF = mybir.ActivationFunctionType
ALU = mybir.AluOpType
AX = mybir.AxisListType

OFFS = ((0, 3, 4, 7), (1, 2, 5, 6))
D = 4096
NOWN = 2048
NLOC = 4096
QSCALE = 1.0 / math.sqrt(128.0)
ENGS = ("te", "act", "dve", "pool", "sync")


class Sched:

    def __init__(self):
        self.events = []
        self.lastw = {}
        self.readers = {}
        self.chans = set()

    def ev(self, eng, fn, reads=(), writes=(), dma=False, force_inc=False, chan=None):
        deps = set()
        for r in reads:
            w = self.lastw.get(r)
            if w is not None:
                deps.add(w)
        for w_ in writes:
            lw = self.lastw.get(w_)
            if lw is not None:
                deps.add(lw)
            for rd in self.readers.get(w_, ()):
                deps.add(rd)
        if dma:
            assert chan is not None
            self.chans.add(chan)
        key = chan if dma else eng
        i = len(self.events)
        self.events.append([eng, fn, deps, dma, force_inc, key])
        for r in reads:
            self.readers.setdefault(r, set()).add(i)
        for w_ in writes:
            self.lastw[w_] = i
            self.readers[w_] = set()
        return i

    def finalize(self, nc, blk, ctx):
        n = len(self.events)
        keys = list(ENGS) + sorted(self.chans)
        sems = {k: ctx.enter_context(nc.semaphore(f"s{j}")) for j, k in enumerate(keys)}
        waits = []
        needed = set()
        for i, (eng, fn, deps, dma, forced, key) in enumerate(self.events):
            bykey = {}
            for d in deps:
                dk = self.events[d][5]
                if dk == "te" and key == "te":
                    continue
                v = bykey.get(dk)
                if v is None or d > v:
                    bykey[dk] = d
            waits.append(bykey)
            for d in bykey.values():
                needed.add(d)
            if forced:
                needed.add(i)
        val_after = [0] * n
        cum = {k: 0 for k in keys}
        for i, (eng, fn, deps, dma, forced, key) in enumerate(self.events):
            if i in needed:
                cum[key] += 16 if dma else 1
            val_after[i] = cum[key]
        per_eng = {e: [] for e in ENGS}
        for i, e in enumerate(self.events):
            per_eng[e[0]].append(i)
        events = self.events

        def mk(e):
            idxs = per_eng[e]

            def emit(engobj):
                water = {x: 0 for x in keys}
                for i in idxs:
                    _, fn, _, dma, _, key = events[i]
                    for dk, d in sorted(waits[i].items(), key=lambda kv: str(kv[0])):
                        v = val_after[d]
                        if v > water[dk]:
                            engobj.wait_ge(sems[dk], v)
                            water[dk] = v
                    inst = fn(engobj)
                    if i in needed:
                        inst.then_inc(sems[key], 16 if dma else 1)
            return emit

        blk.tensor(mk("te"))
        blk.scalar(mk("act"))
        blk.vector(mk("dve"))
        blk.gpsimd(mk("pool"))
        blk.sync(mk("sync"))


def build(phases=("f1a", "f1b", "qa", "qb", "at", "f3"), dbg=None):
    nc = bass.Bass("TRN2", target_bir_lowering=False)

    x = nc.dram_tensor("x", [D, NLOC], F32, kind="ExternalInput").ap()
    wbB = nc.dram_tensor("wbB", [32, 128, 4224], F32, kind="ExternalInput").ap()
    wq4 = nc.dram_tensor("wq4", [32, 128, 4096], F32, kind="ExternalInput").ap()
    wk4 = nc.dram_tensor("wk4", [8, 128, 4096], F32, kind="ExternalInput").ap()
    wv4 = nc.dram_tensor("wv4", [8, 128, 4096], F32, kind="ExternalInput").ap()
    wo4 = nc.dram_tensor("wo4", [32, 128, 4096], F32, kind="ExternalInput").ap()
    gtm = nc.dram_tensor("gtm", [4096, 40], F32, kind="ExternalInput").ap()
    awm = nc.dram_tensor("awm", [128, 4096], F32, kind="ExternalInput").ap()
    bbm = nc.dram_tensor("bbm", [128, 32], F32, kind="ExternalInput").ap()
    bthr = nc.dram_tensor("bthr", [1, 1], F32, kind="ExternalInput").ap()
    maskm = nc.dram_tensor("maskm", [8, 128, 512], F32, kind="ExternalInput").ap()
    idenm = nc.dram_tensor("idenm", [128, 128], F32, kind="ExternalInput").ap()
    onesrm = nc.dram_tensor("onesrm", [1, 128], F32, kind="ExternalInput").ap()
    onescm = nc.dram_tensor("onescm", [128, 1], F32, kind="ExternalInput").ap()
    e16m = nc.dram_tensor("e16m", [8, 128], F32, kind="ExternalInput").ap()

    out = nc.dram_tensor("out", [D, NOWN], F32, kind="ExternalOutput").ap()

    x1s = nc.dram_tensor("x1s", [D, NLOC], F32, kind="Internal").ap()
    x2s = nc.dram_tensor("x2s", [D, NOWN], F32, kind="Internal").ap()
    qs = nc.dram_tensor("qs", [D, NOWN], BF16, kind="Internal").ap()
    ks = nc.dram_tensor("ks", [1024, NLOC], BF16, kind="Internal").ap()
    vs = nc.dram_tensor("vs", [NLOC, 1024], BF16, kind="Internal").ap()
    vs3 = vs.rearrange("(blk pp) c -> pp blk c", pp=128)

    dbg_out = None
    if dbg is not None:
        shp, dt = {"x1s": ([D, NLOC], F32), "x2s": ([D, NOWN], F32),
                   "qs": ([D, NOWN], BF16), "ks": ([1024, NLOC], BF16),
                   "vs": ([NLOC, 1024], BF16)}[dbg]
        dbg_out = nc.dram_tensor("dbg", shp, dt, kind="ExternalOutput").ap()

    with contextlib.ExitStack() as ctx:
        E = ctx.enter_context
        sb = nc.sbuf_tensor
        xnr = E(sb("xnr", [128, 65536], BF16))
        wf = [E(sb(f"wf{i}", [128, 2112], F32)) for i in range(2)]
        wb = [E(sb(f"wb{i}", [128, 4224], BF16)) for i in range(2)]
        qg = E(sb("qg", [128, 2048], BF16))
        mbuf = E(sb("mbuf", [128, 4096], BF16))
        ev = [E(sb(f"ev{i}", [128, 512], F32)) for i in range(4)]
        pt = [E(sb(f"pt{i}", [128, 512], BF16)) for i in range(4)]
        stats = E(sb("stats", [1, 3072], F32))
        gtf = E(sb("gtf", [40, 512], F32))
        sgt = E(sb("sgt", [128, 40], F32))
        gsm = E(sb("gsm", [128, 16], F32))
        e8f = E(sb("e8f", [128, 8], F32))
        e8b = E(sb("e8b", [128, 8], F32))
        w8f = E(sb("w8f", [128, 8], F32))
        wnT = E(sb("wnT", [8, 2048], BF16))
        iden = E(sb("iden", [128, 128], F32))
        onesr = E(sb("onesr", [1, 128], F32))
        onesc32 = E(sb("onesc32", [128, 1], F32))
        onesc = E(sb("onesc", [128, 1], BF16))
        e16b = E(sb("e16b", [8, 128], BF16))
        e16f = E(sb("e16f", [8, 128], F32))
        bbh = E(sb("bbh", [128, 32], F32))
        bthrb = E(sb("bthrb", [128, 1], F32))
        negM = E(sb("negM", [128, 1], F32))
        p = [E(nc.psum_tensor(f"p{i}", [128, 512], F32)) for i in range(8)]
        blk = E(nc.Block())

        sch = Sched()
        S = sch.ev

        def st(i):
            return stats[0:1, i * 512:(i + 1) * 512]

        S("sync", lambda e: e.dma_start(out=iden[:], in_=idenm), writes=[("iden",)], dma=True, chan="c0")
        S("sync", lambda e: e.dma_start(out=onesr[:], in_=onesrm), writes=[("onesr",)], dma=True, chan="c1")
        S("sync", lambda e: e.dma_start(out=onesc32[:], in_=onescm), writes=[("onesc32",)], dma=True, chan="c2")
        S("sync", lambda e: e.dma_start(out=e16f[:], in_=e16m), writes=[("e16f",)], dma=True, chan="c3")
        S("sync", lambda e: e.dma_start(out=ev[1][:, 0:32], in_=bbm), writes=[("ev", 1)], dma=True, chan="c4")
        S("sync", lambda e: e.dma_start(out=gsm[0:1, 0:1], in_=bthr), writes=[("g", 0)], dma=True, chan="c5")
        S("pool", lambda e: e.tensor_copy(out=onesc[:], in_=onesc32[:]),
          reads=[("onesc32",)], writes=[("onesc",)])
        S("pool", lambda e: e.tensor_copy(out=e16b[:], in_=e16f[:]),
          reads=[("e16f",)], writes=[("e16b",)])
        S("pool", lambda e: e.tensor_scalar_mul(bbh[:], ev[1][:, 0:32], 0.5),
          reads=[("ev", 1)], writes=[("bbh",)])
        S("te", lambda e: e.matmul(p[7][:, 0:1], onesr[:], gsm[0:1, 0:1], start=True, stop=True),
          reads=[("g", 0), ("onesr",)], writes=[("p", 7)])
        S("dve", lambda e: e.tensor_copy(out=bthrb[:], in_=p[7][:, 0:1]),
          reads=[("p", 7)], writes=[("bthrb",)])

        def xnslice(ts, dc):
            c = dc * 2048 + ts * 512
            return xnr[:, c:c + 512]

        XNT = [("xnrt", t) for t in range(4)]

        def emit_pre(src_res, src_ap, col_base):
            src3 = src_ap.rearrange("(dc pp) t -> pp dc t", pp=128)
            xnr3 = xnr[:].rearrange("pp (dc t) -> pp dc t", t=2048)
            for s in range(32):
                ts = s // 8
                c0 = col_base + s * 64
                wfP = wf[s % 2]
                w3o = wfP[:, 0:2048].rearrange("pp (dc t) -> pp dc t", t=64)
                S("sync", (lambda e, o=w3o, c=c0: e.dma_start(out=o, in_=src3[:, :, c:c + 64])),
                  reads=list(src_res), writes=[("wf", s % 2)], dma=True, chan=f"cwf{s % 2}")
                cc = ts * 512 + (s % 8) * 64
                o3 = xnr3[:, :, cc:cc + 64]
                S("act", (lambda e, a=o3, b=w3o: e.activation(out=a, in_=b, func=AF.Copy)),
                  reads=[("wf", s % 2)], writes=[("xnrt", ts)])
            for ts in range(4):
                for dc in range(32):
                    sqb = pt[dc % 2]
                    S("act", (lambda e, o=sqb, i=xnslice(ts, dc): e.activation(
                        out=o[:], in_=i, func=AF.Square)),
                        reads=[("xnrt", ts)], writes=[("pt", dc % 2)])

                    def mm2(e, i=xnslice(ts, dc), q=sqb, s0=(dc == 0), s1=(dc == 31)):
                        e.matmul(p[0][0:1, :], onesc[:], i, start=s0, stop=s1)
                        return e.matmul(p[1][0:1, :], onesc[:], q[:], start=s0, stop=s1)
                    S("te", mm2, reads=[("xnrt", ts), ("pt", dc % 2), ("onesc",)],
                      writes=[("p", 0), ("p", 1)])
                S("dve", lambda e: e.tensor_scalar(st(0), p[0][0:1, :], 1.0 / 4096, None, ALU.mult),
                  reads=[("p", 0)], writes=[("st", 0)])
                S("dve", lambda e: e.tensor_scalar(st(1), p[1][0:1, :], 1.0 / 4096, None, ALU.mult),
                  reads=[("p", 1)], writes=[("st", 1)])
                S("dve", lambda e: e.tensor_mul(st(2), st(0), st(0)),
                  reads=[("st", 0)], writes=[("st", 2)])
                S("dve", lambda e: e.scalar_tensor_tensor(
                    out=st(3), in0=st(1), scalar=1e-5, in1=st(2),
                    op0=ALU.add, op1=ALU.subtract),
                  reads=[("st", 1), ("st", 2)], writes=[("st", 3)])
                S("act", lambda e: e.activation(out=st(4), in_=st(3), func=AF.Sqrt),
                  reads=[("st", 3)], writes=[("st", 4)])
                S("dve", lambda e: e.reciprocal(st(5), st(4)),
                  reads=[("st", 4)], writes=[("st", 5)])
                S("dve", lambda e: e.tensor_mul(st(2), st(0), st(5)),
                  reads=[("st", 0), ("st", 5)], writes=[("st", 2)])
                S("te", lambda e: e.matmul(p[2][:], onesr[:], st(5), start=True, stop=True),
                  reads=[("st", 5), ("onesr",)], writes=[("p", 2)])
                S("te", lambda e: e.matmul(p[3][:], onesr[:], st(2), start=True, stop=True),
                  reads=[("st", 2), ("onesr",)], writes=[("p", 3)])
                S("dve", lambda e: e.tensor_copy(out=ev[2][:], in_=p[2][:]),
                  reads=[("p", 2)], writes=[("ev", 2)])
                S("dve", lambda e: e.tensor_copy(out=ev[3][:], in_=p[3][:]),
                  reads=[("p", 3)], writes=[("ev", 3)])
                for dc in range(32):
                    evP = ev[dc % 2]
                    S("dve", (lambda e, o=evP, i=xnslice(ts, dc): e.tensor_mul(o[:], i, ev[2][:])),
                        reads=[("xnrt", ts), ("ev", 2)], writes=[("ev", dc % 2)])
                    S("pool", (lambda e, o=xnslice(ts, dc), i=evP: e.tensor_sub(o, i[:], ev[3][:])),
                        reads=[("ev", dc % 2), ("ev", 3)], writes=[("xnrt", ts)])

        def stream_chunk(src_getter, ci, scale=None, split_B=False):
            P = ci % 2
            wbP = wb[P]
            for h in range(2):
                wfH = wf[h]
                S("sync", (lambda e, a=src_getter(h), w=wfH: e.dma_start(out=w[:, 0:a.shape[1]], in_=a)),
                  writes=[("wf", h)], dma=True, chan=f"cwf{h}")
                if split_B and h == 1:
                    def cst(e, w=wfH, t=wbP):
                        e.activation(out=t[:, 2112:4096], in_=w[:, 0:1984], func=AF.Copy, scale=0.5)
                        return e.activation(out=t[:, 4096:4224], in_=w[:, 1984:2112], func=AF.Copy)
                    S("act", cst, reads=[("wf", h)], writes=[("wbh", P, h)])
                elif split_B and h == 0:
                    S("act", (lambda e, w=wfH, t=wbP: e.activation(
                        out=t[:, 0:2112], in_=w[:, 0:2112], func=AF.Copy, scale=0.5)),
                      reads=[("wf", h)], writes=[("wbh", P, h)])
                elif scale is not None:
                    S("act", (lambda e, w=wfH, t=wbP, hh=h, sc=scale: e.activation(
                        out=t[:, hh * 2048:(hh + 1) * 2048], in_=w[:, 0:2048], func=AF.Copy, scale=sc)),
                        reads=[("wf", h)], writes=[("wbh", P, h)])
                else:
                    S("act", (lambda e, w=wfH, t=wbP, hh=h: e.activation(
                        out=t[:, hh * 2048:(hh + 1) * 2048], in_=w[:, 0:2048], func=AF.Copy)),
                        reads=[("wf", h)], writes=[("wbh", P, h)])
            return wbP, [("wbh", P, 0), ("wbh", P, 1)]

        def emit_gating():
            gt3i = gtm.rearrange("(dc pp) c -> pp dc c", pp=128)
            gt3o = wf[0][:, 0:1280].rearrange("pp (dc c) -> pp dc c", c=40)
            S("sync", lambda e: e.dma_start(out=gt3o, in_=gt3i), writes=[("wf", 0)], dma=True, chan="cwf0")
            S("act", lambda e: e.activation(out=wb[0][:, 0:1280], in_=wf[0][:, 0:1280], func=AF.Copy),
              reads=[("wf", 0)], writes=[("wbh", 0, 0), ("wbh", 0, 1)])

            def gtloop(e):
                for dc in range(32):
                    lh = wb[0][:, dc * 40:(dc + 1) * 40]
                    for ts in range(4):
                        i = e.matmul(p[ts][0:40, :], lh, xnslice(ts, dc),
                                     start=(dc == 0), stop=(dc == 31))
                return i
            S("te", gtloop, reads=[("wbh", 0, 0), ("wbh", 0, 1)] + XNT,
              writes=[("p", 0), ("p", 1), ("p", 2), ("p", 3)])
            S("sync", lambda e: e.dma_start(out=wf[1][:, 0:2048], in_=awm[:, 0:2048]),
              writes=[("wf", 1)], dma=True, chan="cwf1")
            S("act", lambda e: e.activation(out=wb[1][:, 0:2048], in_=wf[1][:, 0:2048], func=AF.Copy),
              reads=[("wf", 1)], writes=[("wbh", 1, 0)])
            S("sync", lambda e: e.dma_start(out=wf[1][:, 0:2048], in_=awm[:, 2048:4096]),
              writes=[("wf", 1)], dma=True, chan="cwf1")
            S("act", lambda e: e.activation(out=wb[1][:, 2048:4096], in_=wf[1][:, 0:2048], func=AF.Copy),
              reads=[("wf", 1)], writes=[("wbh", 1, 1)])
            for ts in range(4):
                S("dve", (lambda e, t=ts: e.tensor_copy(out=gtf[:], in_=p[t][0:40, :])),
                  reads=[("p", ts)], writes=[("gtf", ts)])
                for q in range(4):
                    bank = 6 + (q % 2)
                    S("te", (lambda e, t=ts, qq=q, b=bank: e.transpose(
                        p[b][0:128, 0:40], gtf[0:40, qq * 128:(qq + 1) * 128], iden[0:40, 0:40])),
                        reads=[("gtf", ts), ("iden",)], writes=[("p", bank)])
                    S("dve", (lambda e, b=bank: e.tensor_copy(out=sgt[:], in_=p[b][0:128, 0:40])),
                        reads=[("p", bank)], writes=[("sgt",)])
                    S("dve", lambda e: e.reduce_max(gsm[:, 0:1], sgt[:, 0:8], axis=AX.X),
                      reads=[("sgt",)], writes=[("g", 0)])
                    S("dve", lambda e: e.tensor_scalar_mul(gsm[:, 1:2], gsm[:, 0:1], -1.0),
                      reads=[("g", 0)], writes=[("g", 1)])
                    S("act", lambda e: e.activation(out=e8f[:], in_=sgt[:, 0:8], func=AF.Exp,
                                                    bias=gsm[:, 1:2]),
                      reads=[("sgt",), ("g", 1)], writes=[("e8f",)])
                    S("dve", lambda e: e.reduce_sum(gsm[:, 2:3], e8f[:], axis=AX.X),
                      reads=[("e8f",)], writes=[("g", 2)])
                    S("dve", lambda e: e.reciprocal(gsm[:, 3:4], gsm[:, 2:3]),
                      reads=[("g", 2)], writes=[("g", 3)])
                    S("act", lambda e: e.activation(out=gsm[:, 4:5], in_=sgt[:, 32:33],
                                                    func=AF.Sigmoid, bias=bthrb[:]),
                      reads=[("sgt",), ("bthrb",)], writes=[("g", 4)])
                    S("dve", lambda e: e.tensor_scalar_mul(gsm[:, 5:6], gsm[:, 4:5], 0.125),
                      reads=[("g", 4)], writes=[("g", 5)])
                    S("dve", lambda e: e.tensor_scalar_mul(e8b[:], e8f[:], gsm[:, 3:4]),
                      reads=[("e8f",), ("g", 3)], writes=[("e8b",)])
                    S("dve", lambda e: e.tensor_scalar(w8f[:], e8b[:], gsm[:, 5:6], None, ALU.subtract),
                      reads=[("e8b",), ("g", 5)], writes=[("w8f",)])
                    S("dve", lambda e: e.tensor_scalar_max(e8f[:], w8f[:], 0.0),
                      reads=[("w8f",)], writes=[("e8f",)])
                    S("dve", lambda e: e.reduce_sum(gsm[:, 6:7], e8f[:], axis=AX.X),
                      reads=[("e8f",)], writes=[("g", 6)])
                    S("dve", lambda e: e.tensor_single_scalar(gsm[:, 7:8], gsm[:, 6:7], 0.0, ALU.is_equal),
                      reads=[("g", 6)], writes=[("g", 7)])
                    S("dve", lambda e: e.tensor_add(gsm[:, 8:9], gsm[:, 6:7], gsm[:, 7:8]),
                      reads=[("g", 6), ("g", 7)], writes=[("g", 8)])
                    S("dve", lambda e: e.reciprocal(gsm[:, 9:10], gsm[:, 8:9]),
                      reads=[("g", 8)], writes=[("g", 9)])
                    S("dve", lambda e: e.tensor_scalar_mul(w8f[:], e8f[:], gsm[:, 9:10]),
                      reads=[("e8f",), ("g", 9)], writes=[("w8f",)])
                    S("te", (lambda e, b=bank: e.transpose(p[b][0:8, 0:128], w8f[:, 0:8], iden[:])),
                      reads=[("w8f",), ("iden",)], writes=[("p", bank)])
                    S("dve", (lambda e, t=ts, qq=q, b=bank: e.tensor_copy(
                        out=wnT[0:8, t * 512 + qq * 128: t * 512 + (qq + 1) * 128],
                        in_=p[b][0:8, 0:128])),
                        reads=[("p", bank)], writes=[("wnT", ts)])

            def hfloop(e):
                for dc in range(32):
                    lh = wb[1][:, dc * 128:(dc + 1) * 128]
                    for ts in range(4):
                        i = e.matmul(p[ts][:], lh, xnslice(ts, dc),
                                     start=(dc == 0), stop=(dc == 31))
                return i
            S("te", hfloop,
              reads=[("wbh", 1, 0), ("wbh", 1, 1)] + XNT + [("gtf", t) for t in range(4)],
              writes=[("p", 0), ("p", 1), ("p", 2), ("p", 3)])
            for ts in range(4):
                S("te", (lambda e, t=ts: e.matmul(p[5][:], e16b[:], wnT[0:8, t * 512:(t + 1) * 512],
                                                  start=True, stop=True)),
                  reads=[("wnT", ts), ("e16b",)], writes=[("p", 5)])
                S("dve", lambda e: e.tensor_copy(out=ev[0][:], in_=p[5][:]),
                  reads=[("p", 5)], writes=[("ev", 0)])
                S("dve", (lambda e, t=ts: e.tensor_mul(
                    mbuf[:, t * 512:(t + 1) * 512], p[t][:], ev[0][:])),
                    reads=[("p", t if False else ts), ("ev", 0)], writes=[("mbuf",)])

        def emit_base(res_res, res_ap, res_col, dst_key, dst_ap, dst_col, is_out):
            for oc in range(32):
                wbP, wbres = stream_chunk(
                    (lambda h, o=oc: wbB[o, :, h * 2112:(h + 1) * 2112]), oc, split_B=True)
                P = oc % 2
                base = 4 * P

                def mmloop(e, t=wbP, bb=base):
                    for dc in range(32):
                        lh = t[:, dc * 128:(dc + 1) * 128]
                        for ts in range(4):
                            e.matmul(p[bb + ts][:], lh, xnslice(ts, dc),
                                     start=(dc == 0), stop=False)
                    lhB = t[:, 4096:4224]
                    for ts in range(4):
                        i = e.matmul(p[bb + ts][:], lhB, mbuf[:, ts * 512:(ts + 1) * 512],
                                     start=False, stop=True)
                    return i
                S("te", mmloop, reads=wbres + [("mbuf",)] + XNT,
                  writes=[("p", base), ("p", base + 1), ("p", base + 2), ("p", base + 3)])
                for ts in range(4):
                    eI, eO = ev[ts % 2], ev[2 + ts % 2]
                    ra = res_ap[oc * 128:(oc + 1) * 128,
                                res_col + ts * 512: res_col + (ts + 1) * 512]
                    da = dst_ap[oc * 128:(oc + 1) * 128,
                                dst_col + ts * 512: dst_col + (ts + 1) * 512]
                    S("sync", (lambda e, b=eI, a=ra: e.dma_start(out=b[:], in_=a)),
                      reads=list(res_res), writes=[("ev", ts % 2)], dma=True, chan=f"cev{ts % 2}")
                    S("dve", (lambda e, o=oc, bi=eI, bo=eO, bk=base + ts: e.scalar_tensor_tensor(
                        out=bo[:], in0=p[bk][:], scalar=bbh[:, o:o + 1], in1=bi[:],
                        op0=ALU.add, op1=ALU.add)),
                        reads=[("p", base + ts), ("ev", ts % 2), ("bbh",)],
                        writes=[("ev", 2 + ts % 2)])
                    S("act", (lambda e, b=eO, a=da: e.dma_start(out=a, in_=b[:])),
                      reads=[("ev", 2 + ts % 2)], writes=[(dst_key, ts % 2)], dma=True,
                      force_inc=is_out, chan=f"cst{ts % 2}")

        nrm = {"q": [None], "k": [None]}

        def emit_qkv(grp, do_q, n_k_calls, n_q_calls):
            col = grp * 2048
            chunks = [("k", i) for i in range(8)] + [("v", i) for i in range(8)]
            if do_q:
                chunks += [("q", i) for i in range(32)]
            for ci, (kind, oc) in enumerate(chunks):
                src = {"k": wk4, "v": wv4, "q": wq4}[kind]
                wbP, wbres = stream_chunk(
                    (lambda h, s=src, o=oc: s[o, :, h * 2048:(h + 1) * 2048]), ci,
                    scale=(QSCALE if kind == "q" else None))

                def mmloop(e, t=wbP):
                    for dc in range(32):
                        lh = t[:, dc * 128:(dc + 1) * 128]
                        for ts in range(4):
                            i = e.matmul(p[ts][:], lh, xnslice(ts, dc),
                                         start=(dc == 0), stop=(dc == 31))
                    return i
                S("te", mmloop, reads=wbres + XNT,
                  writes=[("p", 0), ("p", 1), ("p", 2), ("p", 3)])
                for ts in range(4):
                    if kind in ("k", "q"):
                        S("act", (lambda e, t=ts: e.activation(out=pt[0][:], in_=p[t][:], func=AF.Copy)),
                          reads=[("p", ts)], writes=[("pt", 0)])
                        if kind == "k":
                            dsta = ks[oc * 128:(oc + 1) * 128, col + ts * 512: col + (ts + 1) * 512]
                            dres = ("ks",)
                        else:
                            dsta = qs[oc * 128:(oc + 1) * 128, ts * 512:(ts + 1) * 512]
                            dres = ("qs",)
                        S("act", (lambda e, a=dsta: e.dma_start(out=a, in_=pt[0][:])),
                          reads=[("pt", 0)], writes=[dres], dma=True, chan="ckq")
                        S("act", lambda e: e.activation(out=pt[1][:], in_=pt[0][:], func=AF.Square),
                          reads=[("pt", 0)], writes=[("pt", 1)])
                        S("te", lambda e: e.matmul(p[5][0:1, :], onesc[:], pt[1][:],
                                                   start=True, stop=True),
                          reads=[("pt", 1), ("onesc",)], writes=[("p", 5)])
                        S("dve", lambda e: e.reduce_max(gsm[0:1, 15:16], p[5][0:1, :], axis=AX.X),
                          reads=[("p", 5)], writes=[("ntmp",)])
                        cols = (10, 12) if kind == "q" else (11, 13)
                        prev = nrm[kind][0]
                        if prev is None:
                            cur = cols[0]
                            S("dve", (lambda e, cc=cur: e.tensor_copy(
                                out=gsm[0:1, cc:cc + 1], in_=gsm[0:1, 15:16])),
                                reads=[("ntmp",)], writes=[("nrm", kind, cur)])
                        else:
                            cur = cols[0] if prev == cols[1] else cols[1]
                            S("dve", (lambda e, cc=cur, pv=prev: e.tensor_max(
                                gsm[0:1, cc:cc + 1], gsm[0:1, pv:pv + 1], gsm[0:1, 15:16])),
                                reads=[("ntmp",), ("nrm", kind, prev)],
                                writes=[("nrm", kind, cur)])
                        nrm[kind][0] = cur
                    else:
                        evv = ev[2 + ts % 2]
                        S("act", (lambda e, t=ts, o=evv: e.activation(out=o[:], in_=p[t][:], func=AF.Copy)),
                          reads=[("p", ts)], writes=[("ev", 2 + ts % 2)])
                        for q in range(2):
                            bank = (4, 7)[q]

                            def trf(e, qq=q, b=bank, srcb=evv):
                                e.transpose(p[b][0:128, 0:128],
                                            srcb[:, (2 * qq) * 128:(2 * qq + 1) * 128], iden[:])
                                return e.transpose(p[b][0:128, 128:256],
                                                   srcb[:, (2 * qq + 1) * 128:(2 * qq + 2) * 128], iden[:])
                            S("te", trf, reads=[("ev", 2 + ts % 2), ("iden",)], writes=[("p", bank)])
                            S("dve", (lambda e, qq=q, b=bank: e.tensor_copy(
                                out=pt[2][:, (2 * qq) * 128:(2 * qq + 2) * 128],
                                in_=p[b][0:128, 0:256])),
                                reads=[("p", bank)], writes=[("pt", 2)])
                        b0 = col // 128 + ts * 4
                        dsta = vs3[:, b0:b0 + 4, oc * 128:(oc + 1) * 128]
                        S("act", (lambda e, a=dsta: e.dma_start(
                            out=a, in_=pt[2][:].rearrange("pp (q dd) -> pp q dd", dd=128))),
                            reads=[("pt", 2)], writes=[("vs",)], dma=True, chan="cvs")

        def emit_norm_final():
            qc, kc = nrm["q"][0], nrm["k"][0]
            S("dve", (lambda e, a=qc, b=kc: e.tensor_mul(
                gsm[0:1, 14:15], gsm[0:1, a:a + 1], gsm[0:1, b:b + 1])),
                reads=[("nrm", "q", qc), ("nrm", "k", kc)], writes=[("nm",)])
            S("act", lambda e: e.activation(out=gsm[0:1, 15:16], in_=gsm[0:1, 14:15], func=AF.Sqrt),
              reads=[("nm",)], writes=[("nm2",)])
            S("dve", lambda e: e.tensor_scalar_mul(gsm[0:1, 14:15], gsm[0:1, 15:16], -1.0),
              reads=[("nm2",)], writes=[("nm3",)])
            S("te", lambda e: e.matmul(p[4][:, 0:1], onesr[:], gsm[0:1, 14:15], start=True, stop=True),
              reads=[("nm3",), ("onesr",)], writes=[("p", 4)])
            S("dve", lambda e: e.tensor_copy(out=negM[:], in_=p[4][:, 0:1]),
              reads=[("p", 4)], writes=[("negM",)])

        def emit_attn():
            for h in range(2):
                src = maskm[h * 4:(h + 1) * 4].rearrange("j pp q -> pp j q")
                wfH = wf[h]
                S("sync", (lambda e, a=src, w=wfH: e.dma_start(
                    out=w[:, 0:2048].rearrange("pp (j q) -> pp j q", q=512), in_=a)),
                    writes=[("wf", h)], dma=True, chan=f"cwf{h}")
                S("act", (lambda e, w=wfH, hh=h: e.activation(
                    out=mbuf[:, hh * 2048:(hh + 1) * 2048], in_=w[:, 0:2048], func=AF.Copy)),
                    reads=[("wf", h)], writes=[("mbuf",)])
            for kh in range(8):
                S("sync", (lambda e, k=kh: e.dma_start(
                    out=wb[0][:, 0:4096], in_=ks[k * 128:(k + 1) * 128, :])),
                    reads=[("ks",)], writes=[("wbh", 0, 0), ("wbh", 0, 1)], dma=True, chan="cwb0")
                S("sync", (lambda e, k=kh: e.dma_start(
                    out=wb[1][:, 0:4096].rearrange("pp (blk dd) -> pp blk dd", dd=128),
                    in_=vs3[:, :, k * 128:(k + 1) * 128])),
                    reads=[("vs",)], writes=[("wbh", 1, 0), ("wbh", 1, 1)], dma=True, chan="cwb1")
                for gr in range(4):
                    hq = kh * 4 + gr
                    S("sync", (lambda e, h=hq: e.dma_start(
                        out=qg[:], in_=qs[h * 128:(h + 1) * 128, :])),
                        reads=[("qs",)], writes=[("qg",)], dma=True, chan="cqg")
                    for qt in range(4):
                        blocks = []
                        for reg in range(2):
                            for m in range(qt + 1):
                                for jj in range(4):
                                    cb = (reg * 16 + m * 4 + jj) * 128
                                    mslot = (reg * 4 + jj) if m == qt else None
                                    blocks.append((cb, mslot))
                        nb = len(blocks)
                        lav = []

                        def emit_lav(idx):
                            cb, mslot, par, buf = lav[idx]

                            def f(e, c=cb, bu=buf, s0=(idx == 0), s1=(idx == nb - 1)):
                                e.matmul(p[3][0:1, :], onesc[:], bu[:], start=s0, stop=s1)
                                return e.matmul(p[2][:], wb[1][:, c:c + 128], bu[:], start=s0, stop=s1)
                            S("te", f,
                              reads=[("ptt", par) if mslot is None else ("ptm", par),
                                     ("wbh", 1, 0), ("wbh", 1, 1), ("onesc",)],
                              writes=[("p", 2), ("p", 3)])

                        for idx, (cb, mslot) in enumerate(blocks):
                            par = idx % 2
                            S("te", (lambda e, c=cb, pr=par, t=qt: e.matmul(
                                p[pr][:], wb[0][:, c:c + 128], qg[:, t * 512:(t + 1) * 512],
                                start=True, stop=True)),
                                reads=[("wbh", 0, 0), ("wbh", 0, 1), ("qg",)],
                                writes=[("p", par)])
                            S("act", (lambda e, pr=par: e.activation(
                                out=pt[pr][:], in_=p[pr][:], func=AF.Exp, bias=negM[:])),
                                reads=[("p", par), ("negM",)], writes=[("ptt", par)])
                            if mslot is not None:
                                S("pool", (lambda e, pr=par, ms=mslot: e.tensor_mul(
                                    pt[2 + pr][:], pt[pr][:], mbuf[:, ms * 512:(ms + 1) * 512])),
                                    reads=[("ptt", par), ("mbuf",)], writes=[("ptm", par)])
                            lav.append((cb, mslot, par, pt[2 + par] if mslot is not None else pt[par]))
                            if idx >= 1:
                                emit_lav(idx - 1)
                        emit_lav(nb - 1)
                        S("dve", lambda e: e.reciprocal(st(0), p[3][0:1, :]),
                          reads=[("p", 3)], writes=[("st", 0)])
                        S("te", lambda e: e.matmul(p[4][:], onesr[:], st(0), start=True, stop=True),
                          reads=[("st", 0), ("onesr",)], writes=[("p", 4)])
                        S("dve", lambda e: e.tensor_copy(out=ev[2][:], in_=p[4][:]),
                          reads=[("p", 4)], writes=[("ev", 2)])
                        S("dve", (lambda e, h=hq, t=qt: e.tensor_mul(
                            xnr[:, h * 2048 + t * 512: h * 2048 + (t + 1) * 512],
                            p[2][:], ev[2][:])),
                            reads=[("p", 2), ("ev", 2)], writes=[("xnrt", qt)])
            for oc in range(32):
                wbP, wbres = stream_chunk(
                    (lambda h, o=oc: wo4[o, :, h * 2048:(h + 1) * 2048]), oc)
                P = oc % 2
                base = 4 * P

                def mmloop(e, t=wbP, bb=base):
                    for fc in range(32):
                        lh = t[:, fc * 128:(fc + 1) * 128]
                        for ts in range(4):
                            i = e.matmul(p[bb + ts][:], lh,
                                         xnr[:, fc * 2048 + ts * 512: fc * 2048 + (ts + 1) * 512],
                                         start=(fc == 0), stop=(fc == 31))
                    return i
                S("te", mmloop, reads=wbres + XNT,
                  writes=[("p", base), ("p", base + 1), ("p", base + 2), ("p", base + 3)])
                for ts in range(4):
                    eI, eO = ev[ts % 2], ev[2 + ts % 2]
                    S("sync", (lambda e, o=oc, t=ts, b=eI: e.dma_start(
                        out=b[:], in_=x1s[o * 128:(o + 1) * 128, t * 512:(t + 1) * 512])),
                        reads=[("x1s", 0), ("x1s", 1)], writes=[("ev", ts % 2)], dma=True, chan=f"cev{ts % 2}")
                    S("dve", (lambda e, bk=base + ts, bi=eI, bo=eO: e.tensor_add(
                        bo[:], p[bk][:], bi[:])),
                        reads=[("p", base + ts), ("ev", ts % 2)], writes=[("ev", 2 + ts % 2)])
                    S("act", (lambda e, o=oc, t=ts, b=eO: e.dma_start(
                        out=x2s[o * 128:(o + 1) * 128, t * 512:(t + 1) * 512], in_=b[:])),
                        reads=[("ev", 2 + ts % 2)], writes=[("x2s", ts % 2)], dma=True, chan=f"cst{ts % 2}")

        X1S = [("x1s", 0), ("x1s", 1)]
        X2S = [("x2s", 0), ("x2s", 1)]
        if "f1a" in phases:
            emit_pre((), x, 0)
            emit_gating()
            emit_base((), x, 0, "x1s", x1s, 0, False)
        if "f1b" in phases:
            emit_pre((), x, 2048)
            emit_gating()
            emit_base((), x, 2048, "x1s", x1s, 2048, False)
        if "qa" in phases:
            emit_pre(X1S, x1s, 0)
            emit_qkv(0, True, 2, 1)
        if "qb" in phases:
            emit_pre(X1S, x1s, 2048)
            emit_qkv(1, False, 2, 1)
        if "at" in phases:
            emit_norm_final()
            emit_attn()
        if "f3" in phases:
            emit_pre(X2S, x2s, 0)
            emit_gating()
            emit_base(X2S, x2s, 0, "outr", out, 0, True)

        if dbg is not None:
            srcmap = {"x1s": (x1s, [("x1s", 0), ("x1s", 1)]),
                      "x2s": (x2s, [("x2s", 0), ("x2s", 1)]),
                      "qs": (qs, [("qs",)]), "ks": (ks, [("ks",)]), "vs": (vs, [("vs",)])}
            sap, rr = srcmap[dbg]
            nrow = sap.shape[0]
            for i in range(0, nrow, 512):
                S("sync", (lambda e, j=i: e.dma_start(out=dbg_out[j:j + 512, :],
                                                      in_=sap[j:j + 512, :])),
                  reads=rr, writes=[("dbgw",)], dma=True, force_inc=True, chan="cdbg")

        fin = []
        if "f3" in phases:
            fin += [("outr", 0), ("outr", 1)]
        if dbg is not None:
            fin.append(("dbgw",))
        if not fin:
            fin = [("ks",)]
        S("sync", lambda e: e.nop(), reads=fin)

        sch.finalize(nc, blk, ctx)
    return nc


def _chunkmajor(W):
    O, Dd = W.shape
    return np.ascontiguousarray(
        W.reshape(O // 128, 128, Dd // 128, 128).transpose(0, 3, 2, 1).reshape(O // 128, 128, Dd))


def _own_tokens(par):
    return np.concatenate([np.arange((8 * t + o) * 128, (8 * t + o) * 128 + 128)
                           for t in range(4) for o in OFFS[par]])


def _masks(par):
    m = np.zeros((8, 128, 512), np.float32)
    tri = (np.arange(128)[:, None] <= np.arange(128)[None, :]).astype(np.float32)
    for r, offs in enumerate((OFFS[par], OFFS[1 - par])):
        for jj, j in enumerate(offs):
            for i, oi in enumerate(OFFS[par]):
                if j < oi:
                    m[r * 4 + jj][:, i * 128:(i + 1) * 128] = 1.0
                elif j == oi:
                    m[r * 4 + jj][:, i * 128:(i + 1) * 128] = tri
    return m


def prep_inputs(inputs):
    x = np.asarray(inputs["L0_attn_x"], np.float32)
    Wq = np.asarray(inputs["L0_attn_Wq"], np.float32)
    Wk = np.asarray(inputs["L0_attn_Wk"], np.float32)
    Wv = np.asarray(inputs["L0_attn_Wv"], np.float32)
    Wo = np.asarray(inputs["L0_attn_Wo"], np.float32)
    Wb = np.asarray(inputs["L0_ffn_W_base"], np.float32)
    bb = np.asarray(inputs["L0_ffn_b_base"], np.float32)
    Wg = np.asarray(inputs["L0_ffn_W_gate"], np.float32)
    Wt = np.asarray(inputs["L0_ffn_W_thr"], np.float32)
    bt = np.asarray(inputs["L0_ffn_b_thr"], np.float32)
    Aw = np.asarray(inputs["L0_ffn_A_w"], np.float32)
    Bw = np.asarray(inputs["L0_ffn_B_w"], np.float32)

    wb4 = _chunkmajor(Wb)
    Bsb = np.ascontiguousarray(Bw.transpose(1, 0, 2).reshape(4096, 128).T)
    wbB = np.ascontiguousarray(np.concatenate(
        [wb4, Bsb.reshape(128, 32, 128).transpose(1, 0, 2)], axis=2))
    Af = Aw.reshape(128, 4096)
    awm = np.ascontiguousarray(Af.reshape(128, 32, 128).transpose(2, 1, 0).reshape(128, 4096))
    gtm = np.zeros((4096, 40), np.float32)
    gtm[:, 0:8] = Wg.T
    gtm[:, 32] = Wt
    shared = dict(
        wbB=wbB, wq4=_chunkmajor(Wq), wk4=_chunkmajor(Wk), wv4=_chunkmajor(Wv),
        wo4=_chunkmajor(Wo), gtm=gtm, awm=awm,
        bbm=np.ascontiguousarray(bb.reshape(32, 128).T),
        bthr=np.ascontiguousarray(bt.reshape(1, 1)),
        idenm=np.eye(128, dtype=np.float32),
        onesrm=np.ones((1, 128), np.float32),
        onescm=np.ones((128, 1), np.float32),
        e16m=np.kron(np.eye(8, dtype=np.float32), np.ones((1, 16), np.float32)))
    pmask = [_masks(0), _masks(1)]
    in_maps = []
    for c in range(8):
        b, par = c // 2, c % 2
        loc = np.concatenate([_own_tokens(par), _own_tokens(1 - par)])
        d = dict(shared)
        d["x"] = np.ascontiguousarray(x[b][loc].T)
        d["maskm"] = pmask[par]
        in_maps.append(d)
    return in_maps


def kernel(**inputs):
    in_maps = prep_inputs(inputs)
    nc = build()
    res = run_bass_kernel_spmd(nc, in_maps, core_ids=list(range(8)))
    out = np.empty((4, 4096, 4096), np.float32)
    for c in range(8):
        b, par = c // 2, c % 2
        out[b][_own_tokens(par)] = np.ascontiguousarray(res.results[c]["out"]).T
    return out


# version 04
# speedup vs baseline: 1.7529x; 1.1198x over previous
import contextlib
import math
import numpy as np
import ml_dtypes
import concourse.bass as bass
import concourse.mybir as mybir
from concourse.bass_utils import run_bass_kernel_spmd

F32 = mybir.dt.float32
BF16 = mybir.dt.bfloat16
AF = mybir.ActivationFunctionType
ALU = mybir.AluOpType
AX = mybir.AxisListType

OFFS = ((0, 3, 4, 7), (1, 2, 5, 6))
D = 4096
NOWN = 2048
NLOC = 4096
QSCALE = 1.0 / math.sqrt(128.0)
ENGS = ("te", "act", "dve", "pool", "sync")


class Sched:

    def __init__(self):
        self.events = []
        self.lastw = {}
        self.readers = {}
        self.chans = set()

    def ev(self, eng, fn, reads=(), writes=(), dma=False, force_inc=False, chan=None):
        deps = set()
        for r in reads:
            w = self.lastw.get(r)
            if w is not None:
                deps.add(w)
        for w_ in writes:
            lw = self.lastw.get(w_)
            if lw is not None:
                deps.add(lw)
            for rd in self.readers.get(w_, ()):
                deps.add(rd)
        if dma:
            assert chan is not None
            self.chans.add(chan)
        key = chan if dma else eng
        i = len(self.events)
        self.events.append([eng, fn, deps, dma, force_inc, key])
        for r in reads:
            self.readers.setdefault(r, set()).add(i)
        for w_ in writes:
            self.lastw[w_] = i
            self.readers[w_] = set()
        return i

    def finalize(self, nc, blk, ctx):
        n = len(self.events)
        keys = list(ENGS) + sorted(self.chans)
        sems = {k: ctx.enter_context(nc.semaphore(f"s{j}")) for j, k in enumerate(keys)}
        waits = []
        needed = set()
        for i, (eng, fn, deps, dma, forced, key) in enumerate(self.events):
            bykey = {}
            for d in deps:
                dk = self.events[d][5]
                if dk == "te" and key == "te":
                    continue
                v = bykey.get(dk)
                if v is None or d > v:
                    bykey[dk] = d
            waits.append(bykey)
            for d in bykey.values():
                needed.add(d)
            if forced:
                needed.add(i)
        val_after = [0] * n
        cum = {k: 0 for k in keys}
        for i, (eng, fn, deps, dma, forced, key) in enumerate(self.events):
            if i in needed:
                cum[key] += 16 if dma else 1
            val_after[i] = cum[key]
        per_eng = {e: [] for e in ENGS}
        for i, e in enumerate(self.events):
            per_eng[e[0]].append(i)
        events = self.events

        def mk(e):
            idxs = per_eng[e]

            def emit(engobj):
                water = {x: 0 for x in keys}
                for i in idxs:
                    _, fn, _, dma, _, key = events[i]
                    for dk, d in sorted(waits[i].items(), key=lambda kv: str(kv[0])):
                        v = val_after[d]
                        if v > water[dk]:
                            engobj.wait_ge(sems[dk], v)
                            water[dk] = v
                    inst = fn(engobj)
                    if i in needed:
                        inst.then_inc(sems[key], 16 if dma else 1)
            return emit

        blk.tensor(mk("te"))
        blk.scalar(mk("act"))
        blk.vector(mk("dve"))
        blk.gpsimd(mk("pool"))
        blk.sync(mk("sync"))


def build(phases=("f1a", "f1b", "qa", "qb", "at", "f3"), dbg=None):
    nc = bass.Bass("TRN2", target_bir_lowering=False)

    x = nc.dram_tensor("x", [D, NLOC], F32, kind="ExternalInput").ap()
    wbB = nc.dram_tensor("wbB", [32, 128, 4224], F32, kind="ExternalInput").ap()
    wq4 = nc.dram_tensor("wq4", [32, 128, 4096], F32, kind="ExternalInput").ap()
    wk4 = nc.dram_tensor("wk4", [8, 128, 4096], F32, kind="ExternalInput").ap()
    wv4 = nc.dram_tensor("wv4", [8, 128, 4096], F32, kind="ExternalInput").ap()
    wo4 = nc.dram_tensor("wo4", [32, 128, 4096], F32, kind="ExternalInput").ap()
    gtm = nc.dram_tensor("gtm", [4096, 40], F32, kind="ExternalInput").ap()
    awm = nc.dram_tensor("awm", [128, 4096], F32, kind="ExternalInput").ap()
    bbm = nc.dram_tensor("bbm", [128, 32], F32, kind="ExternalInput").ap()
    bthr = nc.dram_tensor("bthr", [1, 1], F32, kind="ExternalInput").ap()
    maskm = nc.dram_tensor("maskm", [8, 128, 512], F32, kind="ExternalInput").ap()
    idenm = nc.dram_tensor("idenm", [128, 128], F32, kind="ExternalInput").ap()
    onesrm = nc.dram_tensor("onesrm", [1, 128], F32, kind="ExternalInput").ap()
    onescm = nc.dram_tensor("onescm", [128, 1], F32, kind="ExternalInput").ap()
    e16m = nc.dram_tensor("e16m", [8, 128], F32, kind="ExternalInput").ap()

    out = nc.dram_tensor("out", [D, NOWN], F32, kind="ExternalOutput").ap()

    x1s = nc.dram_tensor("x1s", [D, NLOC], F32, kind="Internal").ap()
    x2s = nc.dram_tensor("x2s", [D, NOWN], F32, kind="Internal").ap()
    qs = nc.dram_tensor("qs", [D, NOWN], BF16, kind="Internal").ap()
    ks = nc.dram_tensor("ks", [1024, NLOC], BF16, kind="Internal").ap()
    vs = nc.dram_tensor("vs", [NLOC, 1024], BF16, kind="Internal").ap()
    vs3 = vs.rearrange("(blk pp) c -> pp blk c", pp=128)

    dbg_out = None
    if dbg is not None:
        shp, dt = {"x1s": ([D, NLOC], F32), "x2s": ([D, NOWN], F32),
                   "qs": ([D, NOWN], BF16), "ks": ([1024, NLOC], BF16),
                   "vs": ([NLOC, 1024], BF16)}[dbg]
        dbg_out = nc.dram_tensor("dbg", shp, dt, kind="ExternalOutput").ap()

    with contextlib.ExitStack() as ctx:
        E = ctx.enter_context
        sb = nc.sbuf_tensor
        xnr = E(sb("xnr", [128, 65536], BF16))
        wf = [E(sb(f"wf{i}", [128, 2112], F32)) for i in range(2)]
        wb = [E(sb(f"wb{i}", [128, 4224], BF16)) for i in range(2)]
        qg = E(sb("qg", [128, 2048], BF16))
        mbuf = E(sb("mbuf", [128, 4096], BF16))
        ev = [E(sb(f"ev{i}", [128, 512], F32)) for i in range(4)]
        pt = [E(sb(f"pt{i}", [128, 512], BF16)) for i in range(4)]
        stats = E(sb("stats", [1, 3072], F32))
        gtf = E(sb("gtf", [40, 512], F32))
        sgt = E(sb("sgt", [128, 40], F32))
        gsm = E(sb("gsm", [128, 16], F32))
        e8f = E(sb("e8f", [128, 8], F32))
        e8b = E(sb("e8b", [128, 8], F32))
        w8f = E(sb("w8f", [128, 8], F32))
        wnT = E(sb("wnT", [8, 2048], BF16))
        iden = E(sb("iden", [128, 128], F32))
        onesr = E(sb("onesr", [1, 128], F32))
        onesc32 = E(sb("onesc32", [128, 1], F32))
        onesc = E(sb("onesc", [128, 1], BF16))
        e16b = E(sb("e16b", [8, 128], BF16))
        e16f = E(sb("e16f", [8, 128], F32))
        bbh = E(sb("bbh", [128, 32], F32))
        bthrb = E(sb("bthrb", [128, 1], F32))
        negM = E(sb("negM", [128, 1], F32))
        p = [E(nc.psum_tensor(f"p{i}", [128, 512], F32)) for i in range(8)]
        blk = E(nc.Block())

        sch = Sched()
        S = sch.ev

        def st(i):
            return stats[0:1, i * 512:(i + 1) * 512]

        S("sync", lambda e: e.dma_start(out=iden[:], in_=idenm), writes=[("iden",)], dma=True, chan="c0")
        S("sync", lambda e: e.dma_start(out=onesr[:], in_=onesrm), writes=[("onesr",)], dma=True, chan="c1")
        S("sync", lambda e: e.dma_start(out=onesc32[:], in_=onescm), writes=[("onesc32",)], dma=True, chan="c2")
        S("sync", lambda e: e.dma_start(out=e16f[:], in_=e16m), writes=[("e16f",)], dma=True, chan="c3")
        S("sync", lambda e: e.dma_start(out=ev[1][:, 0:32], in_=bbm), writes=[("ev", 1)], dma=True, chan="c4")
        S("sync", lambda e: e.dma_start(out=gsm[0:1, 0:1], in_=bthr), writes=[("g", 0)], dma=True, chan="c5")
        S("pool", lambda e: e.tensor_copy(out=onesc[:], in_=onesc32[:]),
          reads=[("onesc32",)], writes=[("onesc",)])
        S("pool", lambda e: e.tensor_copy(out=e16b[:], in_=e16f[:]),
          reads=[("e16f",)], writes=[("e16b",)])
        S("pool", lambda e: e.tensor_scalar_mul(bbh[:], ev[1][:, 0:32], 0.5),
          reads=[("ev", 1)], writes=[("bbh",)])
        S("te", lambda e: e.matmul(p[7][:, 0:1], onesr[:], gsm[0:1, 0:1], start=True, stop=True),
          reads=[("g", 0), ("onesr",)], writes=[("p", 7)])
        S("dve", lambda e: e.tensor_copy(out=bthrb[:], in_=p[7][:, 0:1]),
          reads=[("p", 7)], writes=[("bthrb",)])

        def xnslice(ts, dc):
            c = dc * 2048 + ts * 512
            return xnr[:, c:c + 512]

        XNT = [("xnrt", t) for t in range(4)]

        def emit_pre(src_res, src_ap, col_base):
            src3 = src_ap.rearrange("(dc pp) t -> pp dc t", pp=128)
            xnr3 = xnr[:].rearrange("pp (dc t) -> pp dc t", t=2048)
            for s in range(32):
                ts = s // 8
                c0 = col_base + s * 64
                wfP = wf[s % 2]
                w3o = wfP[:, 0:2048].rearrange("pp (dc t) -> pp dc t", t=64)
                S("sync", (lambda e, o=w3o, c=c0: e.dma_start(out=o, in_=src3[:, :, c:c + 64])),
                  reads=list(src_res), writes=[("wf", s % 2)], dma=True, chan=f"cwf{s % 2}")
                cc = ts * 512 + (s % 8) * 64
                o3 = xnr3[:, :, cc:cc + 64]
                S("act", (lambda e, a=o3, b=w3o: e.activation(out=a, in_=b, func=AF.Copy)),
                  reads=[("wf", s % 2)], writes=[("xnrt", ts)])
            for ts in range(4):
                for dc in range(32):
                    sqb = pt[dc % 2]
                    S("act", (lambda e, o=sqb, i=xnslice(ts, dc): e.activation(
                        out=o[:], in_=i, func=AF.Square)),
                        reads=[("xnrt", ts)], writes=[("pt", dc % 2)])

                    def mm2(e, i=xnslice(ts, dc), q=sqb, s0=(dc == 0), s1=(dc == 31)):
                        e.matmul(p[0][0:1, :], onesc[:], i, start=s0, stop=s1)
                        return e.matmul(p[1][0:1, :], onesc[:], q[:], start=s0, stop=s1)
                    S("te", mm2, reads=[("xnrt", ts), ("pt", dc % 2), ("onesc",)],
                      writes=[("p", 0), ("p", 1)])
                S("dve", lambda e: e.tensor_scalar(st(0), p[0][0:1, :], 1.0 / 4096, None, ALU.mult),
                  reads=[("p", 0)], writes=[("st", 0)])
                S("dve", lambda e: e.tensor_scalar(st(1), p[1][0:1, :], 1.0 / 4096, None, ALU.mult),
                  reads=[("p", 1)], writes=[("st", 1)])
                S("dve", lambda e: e.tensor_mul(st(2), st(0), st(0)),
                  reads=[("st", 0)], writes=[("st", 2)])
                S("dve", lambda e: e.scalar_tensor_tensor(
                    out=st(3), in0=st(1), scalar=1e-5, in1=st(2),
                    op0=ALU.add, op1=ALU.subtract),
                  reads=[("st", 1), ("st", 2)], writes=[("st", 3)])
                S("act", lambda e: e.activation(out=st(4), in_=st(3), func=AF.Sqrt),
                  reads=[("st", 3)], writes=[("st", 4)])
                S("dve", lambda e: e.reciprocal(st(5), st(4)),
                  reads=[("st", 4)], writes=[("st", 5)])
                S("dve", lambda e: e.tensor_mul(st(2), st(0), st(5)),
                  reads=[("st", 0), ("st", 5)], writes=[("st", 2)])
                S("te", lambda e: e.matmul(p[2][:], onesr[:], st(5), start=True, stop=True),
                  reads=[("st", 5), ("onesr",)], writes=[("p", 2)])
                S("te", lambda e: e.matmul(p[3][:], onesr[:], st(2), start=True, stop=True),
                  reads=[("st", 2), ("onesr",)], writes=[("p", 3)])
                S("dve", lambda e: e.tensor_copy(out=ev[2][:], in_=p[2][:]),
                  reads=[("p", 2)], writes=[("ev", 2)])
                S("dve", lambda e: e.tensor_copy(out=ev[3][:], in_=p[3][:]),
                  reads=[("p", 3)], writes=[("ev", 3)])
                for dc in range(32):
                    evP = ev[dc % 2]
                    S("dve", (lambda e, o=evP, i=xnslice(ts, dc): e.tensor_mul(o[:], i, ev[2][:])),
                        reads=[("xnrt", ts), ("ev", 2)], writes=[("ev", dc % 2)])
                    S("pool", (lambda e, o=xnslice(ts, dc), i=evP: e.tensor_sub(o, i[:], ev[3][:])),
                        reads=[("ev", dc % 2), ("ev", 3)], writes=[("xnrt", ts)])

        def stream_chunk(src_getter, ci, scale=None, split_B=False):
            P = ci % 2
            wbP = wb[P]
            for h in range(2):
                wfH = wf[h]
                S("sync", (lambda e, a=src_getter(h), w=wfH: e.dma_start(out=w[:, 0:a.shape[1]], in_=a)),
                  writes=[("wf", h)], dma=True, chan=f"cwf{h}")
                if split_B and h == 1:
                    def cst(e, w=wfH, t=wbP):
                        e.activation(out=t[:, 2112:4096], in_=w[:, 0:1984], func=AF.Copy, scale=0.5)
                        return e.activation(out=t[:, 4096:4224], in_=w[:, 1984:2112], func=AF.Copy)
                    S("act", cst, reads=[("wf", h)], writes=[("wbh", P, h)])
                elif split_B and h == 0:
                    S("act", (lambda e, w=wfH, t=wbP: e.activation(
                        out=t[:, 0:2112], in_=w[:, 0:2112], func=AF.Copy, scale=0.5)),
                      reads=[("wf", h)], writes=[("wbh", P, h)])
                elif scale is not None:
                    S("act", (lambda e, w=wfH, t=wbP, hh=h, sc=scale: e.activation(
                        out=t[:, hh * 2048:(hh + 1) * 2048], in_=w[:, 0:2048], func=AF.Copy, scale=sc)),
                        reads=[("wf", h)], writes=[("wbh", P, h)])
                else:
                    S("act", (lambda e, w=wfH, t=wbP, hh=h: e.activation(
                        out=t[:, hh * 2048:(hh + 1) * 2048], in_=w[:, 0:2048], func=AF.Copy)),
                        reads=[("wf", h)], writes=[("wbh", P, h)])
            return wbP, [("wbh", P, 0), ("wbh", P, 1)]

        def emit_gating():
            gt3i = gtm.rearrange("(dc pp) c -> pp dc c", pp=128)
            gt3o = wf[0][:, 0:1280].rearrange("pp (dc c) -> pp dc c", c=40)
            S("sync", lambda e: e.dma_start(out=gt3o, in_=gt3i), writes=[("wf", 0)], dma=True, chan="cwf0")
            S("act", lambda e: e.activation(out=wb[0][:, 0:1280], in_=wf[0][:, 0:1280], func=AF.Copy),
              reads=[("wf", 0)], writes=[("wbh", 0, 0), ("wbh", 0, 1)])

            def gtloop(e):
                for dc in range(32):
                    lh = wb[0][:, dc * 40:(dc + 1) * 40]
                    for ts in range(4):
                        i = e.matmul(p[ts][0:40, :], lh, xnslice(ts, dc),
                                     start=(dc == 0), stop=(dc == 31))
                return i
            S("te", gtloop, reads=[("wbh", 0, 0), ("wbh", 0, 1)] + XNT,
              writes=[("p", 0), ("p", 1), ("p", 2), ("p", 3)])
            S("sync", lambda e: e.dma_start(out=wf[1][:, 0:2048], in_=awm[:, 0:2048]),
              writes=[("wf", 1)], dma=True, chan="cwf1")
            S("act", lambda e: e.activation(out=wb[1][:, 0:2048], in_=wf[1][:, 0:2048], func=AF.Copy),
              reads=[("wf", 1)], writes=[("wbh", 1, 0)])
            S("sync", lambda e: e.dma_start(out=wf[1][:, 0:2048], in_=awm[:, 2048:4096]),
              writes=[("wf", 1)], dma=True, chan="cwf1")
            S("act", lambda e: e.activation(out=wb[1][:, 2048:4096], in_=wf[1][:, 0:2048], func=AF.Copy),
              reads=[("wf", 1)], writes=[("wbh", 1, 1)])
            for ts in range(4):
                S("dve", (lambda e, t=ts: e.tensor_copy(out=gtf[:], in_=p[t][0:40, :])),
                  reads=[("p", ts)], writes=[("gtf", ts)])
                for q in range(4):
                    bank = 6 + (q % 2)
                    S("te", (lambda e, t=ts, qq=q, b=bank: e.transpose(
                        p[b][0:128, 0:40], gtf[0:40, qq * 128:(qq + 1) * 128], iden[0:40, 0:40])),
                        reads=[("gtf", ts), ("iden",)], writes=[("p", bank)])
                    S("dve", (lambda e, b=bank: e.tensor_copy(out=sgt[:], in_=p[b][0:128, 0:40])),
                        reads=[("p", bank)], writes=[("sgt",)])
                    S("dve", lambda e: e.reduce_max(gsm[:, 0:1], sgt[:, 0:8], axis=AX.X),
                      reads=[("sgt",)], writes=[("g", 0)])
                    S("dve", lambda e: e.tensor_scalar_mul(gsm[:, 1:2], gsm[:, 0:1], -1.0),
                      reads=[("g", 0)], writes=[("g", 1)])
                    S("act", lambda e: e.activation(out=e8f[:], in_=sgt[:, 0:8], func=AF.Exp,
                                                    bias=gsm[:, 1:2]),
                      reads=[("sgt",), ("g", 1)], writes=[("e8f",)])
                    S("dve", lambda e: e.reduce_sum(gsm[:, 2:3], e8f[:], axis=AX.X),
                      reads=[("e8f",)], writes=[("g", 2)])
                    S("dve", lambda e: e.reciprocal(gsm[:, 3:4], gsm[:, 2:3]),
                      reads=[("g", 2)], writes=[("g", 3)])
                    S("act", lambda e: e.activation(out=gsm[:, 4:5], in_=sgt[:, 32:33],
                                                    func=AF.Sigmoid, bias=bthrb[:]),
                      reads=[("sgt",), ("bthrb",)], writes=[("g", 4)])
                    S("dve", lambda e: e.tensor_scalar_mul(gsm[:, 5:6], gsm[:, 4:5], 0.125),
                      reads=[("g", 4)], writes=[("g", 5)])
                    S("dve", lambda e: e.tensor_scalar_mul(e8b[:], e8f[:], gsm[:, 3:4]),
                      reads=[("e8f",), ("g", 3)], writes=[("e8b",)])
                    S("dve", lambda e: e.tensor_scalar(w8f[:], e8b[:], gsm[:, 5:6], None, ALU.subtract),
                      reads=[("e8b",), ("g", 5)], writes=[("w8f",)])
                    S("dve", lambda e: e.tensor_scalar_max(e8f[:], w8f[:], 0.0),
                      reads=[("w8f",)], writes=[("e8f",)])
                    S("dve", lambda e: e.reduce_sum(gsm[:, 6:7], e8f[:], axis=AX.X),
                      reads=[("e8f",)], writes=[("g", 6)])
                    S("dve", lambda e: e.tensor_single_scalar(gsm[:, 7:8], gsm[:, 6:7], 0.0, ALU.is_equal),
                      reads=[("g", 6)], writes=[("g", 7)])
                    S("dve", lambda e: e.tensor_add(gsm[:, 8:9], gsm[:, 6:7], gsm[:, 7:8]),
                      reads=[("g", 6), ("g", 7)], writes=[("g", 8)])
                    S("dve", lambda e: e.reciprocal(gsm[:, 9:10], gsm[:, 8:9]),
                      reads=[("g", 8)], writes=[("g", 9)])
                    S("dve", lambda e: e.tensor_scalar_mul(w8f[:], e8f[:], gsm[:, 9:10]),
                      reads=[("e8f",), ("g", 9)], writes=[("w8f",)])
                    S("te", (lambda e, b=bank: e.transpose(p[b][0:8, 0:128], w8f[:, 0:8], iden[:])),
                      reads=[("w8f",), ("iden",)], writes=[("p", bank)])
                    S("dve", (lambda e, t=ts, qq=q, b=bank: e.tensor_copy(
                        out=wnT[0:8, t * 512 + qq * 128: t * 512 + (qq + 1) * 128],
                        in_=p[b][0:8, 0:128])),
                        reads=[("p", bank)], writes=[("wnT", ts)])

            def hfloop(e):
                for dc in range(32):
                    lh = wb[1][:, dc * 128:(dc + 1) * 128]
                    for ts in range(4):
                        i = e.matmul(p[ts][:], lh, xnslice(ts, dc),
                                     start=(dc == 0), stop=(dc == 31))
                return i
            S("te", hfloop,
              reads=[("wbh", 1, 0), ("wbh", 1, 1)] + XNT + [("gtf", t) for t in range(4)],
              writes=[("p", 0), ("p", 1), ("p", 2), ("p", 3)])
            for ts in range(4):
                S("te", (lambda e, t=ts: e.matmul(p[5][:], e16b[:], wnT[0:8, t * 512:(t + 1) * 512],
                                                  start=True, stop=True)),
                  reads=[("wnT", ts), ("e16b",)], writes=[("p", 5)])
                S("dve", lambda e: e.tensor_copy(out=ev[0][:], in_=p[5][:]),
                  reads=[("p", 5)], writes=[("ev", 0)])
                S("dve", (lambda e, t=ts: e.tensor_mul(
                    mbuf[:, t * 512:(t + 1) * 512], p[t][:], ev[0][:])),
                    reads=[("p", t if False else ts), ("ev", 0)], writes=[("mbuf",)])

        def emit_base(res_res, res_ap, res_col, dst_key, dst_ap, dst_col, is_out):
            pend = {0: stream_chunk((lambda h: wbB[0, :, h * 2112:(h + 1) * 2112]), 0, split_B=True)}
            for oc in range(32):
                wbP, wbres = pend.pop(oc)
                if oc + 1 < 32:
                    pend[oc + 1] = stream_chunk(
                        (lambda h, o=oc + 1: wbB[o, :, h * 2112:(h + 1) * 2112]),
                        oc + 1, split_B=True)
                P = oc % 2
                base = 4 * P

                def mmloop(e, t=wbP, bb=base):
                    for dc in range(32):
                        lh = t[:, dc * 128:(dc + 1) * 128]
                        for ts in range(4):
                            e.matmul(p[bb + ts][:], lh, xnslice(ts, dc),
                                     start=(dc == 0), stop=False)
                    lhB = t[:, 4096:4224]
                    for ts in range(4):
                        i = e.matmul(p[bb + ts][:], lhB, mbuf[:, ts * 512:(ts + 1) * 512],
                                     start=False, stop=True)
                    return i
                S("te", mmloop, reads=wbres + [("mbuf",)] + XNT,
                  writes=[("p", base), ("p", base + 1), ("p", base + 2), ("p", base + 3)])
                for ts in range(4):
                    eI, eO = ev[ts % 2], ev[2 + ts % 2]
                    ra = res_ap[oc * 128:(oc + 1) * 128,
                                res_col + ts * 512: res_col + (ts + 1) * 512]
                    da = dst_ap[oc * 128:(oc + 1) * 128,
                                dst_col + ts * 512: dst_col + (ts + 1) * 512]
                    S("sync", (lambda e, b=eI, a=ra: e.dma_start(out=b[:], in_=a)),
                      reads=list(res_res), writes=[("ev", ts % 2)], dma=True, chan=f"cev{ts % 2}")
                    S("dve", (lambda e, o=oc, bi=eI, bo=eO, bk=base + ts: e.scalar_tensor_tensor(
                        out=bo[:], in0=p[bk][:], scalar=bbh[:, o:o + 1], in1=bi[:],
                        op0=ALU.add, op1=ALU.add)),
                        reads=[("p", base + ts), ("ev", ts % 2), ("bbh",)],
                        writes=[("ev", 2 + ts % 2)])
                    S("act", (lambda e, b=eO, a=da: e.dma_start(out=a, in_=b[:])),
                      reads=[("ev", 2 + ts % 2)], writes=[(dst_key, ts % 2)], dma=True,
                      force_inc=is_out, chan=f"cst{ts % 2}")

        nrm = {"q": [None], "k": [None]}

        def emit_qkv(grp, do_q, n_k_calls, n_q_calls):
            col = grp * 2048
            chunks = [("k", i) for i in range(8)] + [("v", i) for i in range(8)]
            if do_q:
                chunks += [("q", i) for i in range(32)]
            srcs = {"k": wk4, "v": wv4, "q": wq4}

            def qkv_stream(ci):
                kind, oc = chunks[ci]
                return stream_chunk(
                    (lambda h, s=srcs[kind], o=oc: s[o, :, h * 2048:(h + 1) * 2048]), ci,
                    scale=(QSCALE if kind == "q" else None))
            pend = {0: qkv_stream(0)}
            for ci, (kind, oc) in enumerate(chunks):
                wbP, wbres = pend.pop(ci)
                if ci + 1 < len(chunks):
                    pend[ci + 1] = qkv_stream(ci + 1)

                def mmloop(e, t=wbP):
                    for dc in range(32):
                        lh = t[:, dc * 128:(dc + 1) * 128]
                        for ts in range(4):
                            i = e.matmul(p[ts][:], lh, xnslice(ts, dc),
                                         start=(dc == 0), stop=(dc == 31))
                    return i
                S("te", mmloop, reads=wbres + XNT,
                  writes=[("p", 0), ("p", 1), ("p", 2), ("p", 3)])
                for ts in range(4):
                    if kind in ("k", "q"):
                        S("act", (lambda e, t=ts: e.activation(out=pt[0][:], in_=p[t][:], func=AF.Copy)),
                          reads=[("p", ts)], writes=[("pt", 0)])
                        if kind == "k":
                            dsta = ks[oc * 128:(oc + 1) * 128, col + ts * 512: col + (ts + 1) * 512]
                            dres = ("ks",)
                        else:
                            dsta = qs[oc * 128:(oc + 1) * 128, ts * 512:(ts + 1) * 512]
                            dres = ("qs",)
                        S("act", (lambda e, a=dsta: e.dma_start(out=a, in_=pt[0][:])),
                          reads=[("pt", 0)], writes=[dres], dma=True, chan="ckq")
                        S("act", lambda e: e.activation(out=pt[1][:], in_=pt[0][:], func=AF.Square),
                          reads=[("pt", 0)], writes=[("pt", 1)])
                        S("te", lambda e: e.matmul(p[5][0:1, :], onesc[:], pt[1][:],
                                                   start=True, stop=True),
                          reads=[("pt", 1), ("onesc",)], writes=[("p", 5)])
                        S("dve", lambda e: e.reduce_max(gsm[0:1, 15:16], p[5][0:1, :], axis=AX.X),
                          reads=[("p", 5)], writes=[("ntmp",)])
                        cols = (10, 12) if kind == "q" else (11, 13)
                        prev = nrm[kind][0]
                        if prev is None:
                            cur = cols[0]
                            S("dve", (lambda e, cc=cur: e.tensor_copy(
                                out=gsm[0:1, cc:cc + 1], in_=gsm[0:1, 15:16])),
                                reads=[("ntmp",)], writes=[("nrm", kind, cur)])
                        else:
                            cur = cols[0] if prev == cols[1] else cols[1]
                            S("dve", (lambda e, cc=cur, pv=prev: e.tensor_max(
                                gsm[0:1, cc:cc + 1], gsm[0:1, pv:pv + 1], gsm[0:1, 15:16])),
                                reads=[("ntmp",), ("nrm", kind, prev)],
                                writes=[("nrm", kind, cur)])
                        nrm[kind][0] = cur
                    else:
                        evv = ev[2 + ts % 2]
                        S("act", (lambda e, t=ts, o=evv: e.activation(out=o[:], in_=p[t][:], func=AF.Copy)),
                          reads=[("p", ts)], writes=[("ev", 2 + ts % 2)])
                        for q in range(2):
                            bank = (4, 7)[q]

                            def trf(e, qq=q, b=bank, srcb=evv):
                                e.transpose(p[b][0:128, 0:128],
                                            srcb[:, (2 * qq) * 128:(2 * qq + 1) * 128], iden[:])
                                return e.transpose(p[b][0:128, 128:256],
                                                   srcb[:, (2 * qq + 1) * 128:(2 * qq + 2) * 128], iden[:])
                            S("te", trf, reads=[("ev", 2 + ts % 2), ("iden",)], writes=[("p", bank)])
                            S("dve", (lambda e, qq=q, b=bank: e.tensor_copy(
                                out=pt[2][:, (2 * qq) * 128:(2 * qq + 2) * 128],
                                in_=p[b][0:128, 0:256])),
                                reads=[("p", bank)], writes=[("pt", 2)])
                        b0 = col // 128 + ts * 4
                        dsta = vs3[:, b0:b0 + 4, oc * 128:(oc + 1) * 128]
                        S("act", (lambda e, a=dsta: e.dma_start(
                            out=a, in_=pt[2][:].rearrange("pp (q dd) -> pp q dd", dd=128))),
                            reads=[("pt", 2)], writes=[("vs",)], dma=True, chan="cvs")

        def emit_norm_final():
            qc, kc = nrm["q"][0], nrm["k"][0]
            S("dve", (lambda e, a=qc, b=kc: e.tensor_mul(
                gsm[0:1, 14:15], gsm[0:1, a:a + 1], gsm[0:1, b:b + 1])),
                reads=[("nrm", "q", qc), ("nrm", "k", kc)], writes=[("nm",)])
            S("act", lambda e: e.activation(out=gsm[0:1, 15:16], in_=gsm[0:1, 14:15], func=AF.Sqrt),
              reads=[("nm",)], writes=[("nm2",)])
            S("dve", lambda e: e.tensor_scalar_mul(gsm[0:1, 14:15], gsm[0:1, 15:16], -1.0),
              reads=[("nm2",)], writes=[("nm3",)])
            S("te", lambda e: e.matmul(p[4][:, 0:1], onesr[:], gsm[0:1, 14:15], start=True, stop=True),
              reads=[("nm3",), ("onesr",)], writes=[("p", 4)])
            S("dve", lambda e: e.tensor_copy(out=negM[:], in_=p[4][:, 0:1]),
              reads=[("p", 4)], writes=[("negM",)])

        def emit_attn():
            for h in range(2):
                src = maskm[h * 4:(h + 1) * 4].rearrange("j pp q -> pp j q")
                wfH = wf[h]
                S("sync", (lambda e, a=src, w=wfH: e.dma_start(
                    out=w[:, 0:2048].rearrange("pp (j q) -> pp j q", q=512), in_=a)),
                    writes=[("wf", h)], dma=True, chan=f"cwf{h}")
                S("act", (lambda e, w=wfH, hh=h: e.activation(
                    out=mbuf[:, hh * 2048:(hh + 1) * 2048], in_=w[:, 0:2048], func=AF.Copy)),
                    reads=[("wf", h)], writes=[("mbuf",)])
            for kh in range(8):
                S("sync", (lambda e, k=kh: e.dma_start(
                    out=wb[0][:, 0:4096], in_=ks[k * 128:(k + 1) * 128, :])),
                    reads=[("ks",)], writes=[("wbh", 0, 0), ("wbh", 0, 1)], dma=True, chan="cwb0")
                S("sync", (lambda e, k=kh: e.dma_start(
                    out=wb[1][:, 0:4096].rearrange("pp (blk dd) -> pp blk dd", dd=128),
                    in_=vs3[:, :, k * 128:(k + 1) * 128])),
                    reads=[("vs",)], writes=[("wbh", 1, 0), ("wbh", 1, 1)], dma=True, chan="cwb1")
                for gr in range(4):
                    hq = kh * 4 + gr
                    S("sync", (lambda e, h=hq: e.dma_start(
                        out=qg[:], in_=qs[h * 128:(h + 1) * 128, :])),
                        reads=[("qs",)], writes=[("qg",)], dma=True, chan="cqg")
                    for qt in range(4):
                        blocks = []
                        for reg in range(2):
                            for m in range(qt + 1):
                                for jj in range(4):
                                    cb = (reg * 16 + m * 4 + jj) * 128
                                    mslot = (reg * 4 + jj) if m == qt else None
                                    blocks.append((cb, mslot))
                        nb = len(blocks)
                        lav = []

                        def emit_lav(idx):
                            cb, mslot, par, buf = lav[idx]

                            def f(e, c=cb, bu=buf, s0=(idx == 0), s1=(idx == nb - 1)):
                                e.matmul(p[3][0:1, :], onesc[:], bu[:], start=s0, stop=s1)
                                return e.matmul(p[2][:], wb[1][:, c:c + 128], bu[:], start=s0, stop=s1)
                            S("te", f,
                              reads=[("ptt", par) if mslot is None else ("ptm", par),
                                     ("wbh", 1, 0), ("wbh", 1, 1), ("onesc",)],
                              writes=[("p", 2), ("p", 3)])

                        for idx, (cb, mslot) in enumerate(blocks):
                            par = idx % 2
                            S("te", (lambda e, c=cb, pr=par, t=qt: e.matmul(
                                p[pr][:], wb[0][:, c:c + 128], qg[:, t * 512:(t + 1) * 512],
                                start=True, stop=True)),
                                reads=[("wbh", 0, 0), ("wbh", 0, 1), ("qg",)],
                                writes=[("p", par)])
                            S("act", (lambda e, pr=par: e.activation(
                                out=pt[pr][:], in_=p[pr][:], func=AF.Exp, bias=negM[:])),
                                reads=[("p", par), ("negM",)], writes=[("ptt", par)])
                            if mslot is not None:
                                S("pool", (lambda e, pr=par, ms=mslot: e.tensor_mul(
                                    pt[2 + pr][:], pt[pr][:], mbuf[:, ms * 512:(ms + 1) * 512])),
                                    reads=[("ptt", par), ("mbuf",)], writes=[("ptm", par)])
                            lav.append((cb, mslot, par, pt[2 + par] if mslot is not None else pt[par]))
                            if idx >= 1:
                                emit_lav(idx - 1)
                        emit_lav(nb - 1)
                        S("dve", lambda e: e.reciprocal(st(0), p[3][0:1, :]),
                          reads=[("p", 3)], writes=[("st", 0)])
                        S("te", lambda e: e.matmul(p[4][:], onesr[:], st(0), start=True, stop=True),
                          reads=[("st", 0), ("onesr",)], writes=[("p", 4)])
                        S("dve", lambda e: e.tensor_copy(out=ev[2][:], in_=p[4][:]),
                          reads=[("p", 4)], writes=[("ev", 2)])
                        S("dve", (lambda e, h=hq, t=qt: e.tensor_mul(
                            xnr[:, h * 2048 + t * 512: h * 2048 + (t + 1) * 512],
                            p[2][:], ev[2][:])),
                            reads=[("p", 2), ("ev", 2)], writes=[("xnrt", qt)])
            pend = {0: stream_chunk((lambda h: wo4[0, :, h * 2048:(h + 1) * 2048]), 0)}
            for oc in range(32):
                wbP, wbres = pend.pop(oc)
                if oc + 1 < 32:
                    pend[oc + 1] = stream_chunk(
                        (lambda h, o=oc + 1: wo4[o, :, h * 2048:(h + 1) * 2048]), oc + 1)
                P = oc % 2
                base = 4 * P

                def mmloop(e, t=wbP, bb=base):
                    for fc in range(32):
                        lh = t[:, fc * 128:(fc + 1) * 128]
                        for ts in range(4):
                            i = e.matmul(p[bb + ts][:], lh,
                                         xnr[:, fc * 2048 + ts * 512: fc * 2048 + (ts + 1) * 512],
                                         start=(fc == 0), stop=(fc == 31))
                    return i
                S("te", mmloop, reads=wbres + XNT,
                  writes=[("p", base), ("p", base + 1), ("p", base + 2), ("p", base + 3)])
                for ts in range(4):
                    eI, eO = ev[ts % 2], ev[2 + ts % 2]
                    S("sync", (lambda e, o=oc, t=ts, b=eI: e.dma_start(
                        out=b[:], in_=x1s[o * 128:(o + 1) * 128, t * 512:(t + 1) * 512])),
                        reads=[("x1s", 0), ("x1s", 1)], writes=[("ev", ts % 2)], dma=True, chan=f"cev{ts % 2}")
                    S("dve", (lambda e, bk=base + ts, bi=eI, bo=eO: e.tensor_add(
                        bo[:], p[bk][:], bi[:])),
                        reads=[("p", base + ts), ("ev", ts % 2)], writes=[("ev", 2 + ts % 2)])
                    S("act", (lambda e, o=oc, t=ts, b=eO: e.dma_start(
                        out=x2s[o * 128:(o + 1) * 128, t * 512:(t + 1) * 512], in_=b[:])),
                        reads=[("ev", 2 + ts % 2)], writes=[("x2s", ts % 2)], dma=True, chan=f"cst{ts % 2}")

        X1S = [("x1s", 0), ("x1s", 1)]
        X2S = [("x2s", 0), ("x2s", 1)]
        if "f1a" in phases:
            emit_pre((), x, 0)
            emit_gating()
            emit_base((), x, 0, "x1s", x1s, 0, False)
        if "f1b" in phases:
            emit_pre((), x, 2048)
            emit_gating()
            emit_base((), x, 2048, "x1s", x1s, 2048, False)
        if "qa" in phases:
            emit_pre(X1S, x1s, 0)
            emit_qkv(0, True, 2, 1)
        if "qb" in phases:
            emit_pre(X1S, x1s, 2048)
            emit_qkv(1, False, 2, 1)
        if "at" in phases:
            emit_norm_final()
            emit_attn()
        if "f3" in phases:
            emit_pre(X2S, x2s, 0)
            emit_gating()
            emit_base(X2S, x2s, 0, "outr", out, 0, True)

        if dbg is not None:
            srcmap = {"x1s": (x1s, [("x1s", 0), ("x1s", 1)]),
                      "x2s": (x2s, [("x2s", 0), ("x2s", 1)]),
                      "qs": (qs, [("qs",)]), "ks": (ks, [("ks",)]), "vs": (vs, [("vs",)])}
            sap, rr = srcmap[dbg]
            nrow = sap.shape[0]
            for i in range(0, nrow, 512):
                S("sync", (lambda e, j=i: e.dma_start(out=dbg_out[j:j + 512, :],
                                                      in_=sap[j:j + 512, :])),
                  reads=rr, writes=[("dbgw",)], dma=True, force_inc=True, chan="cdbg")

        fin = []
        if "f3" in phases:
            fin += [("outr", 0), ("outr", 1)]
        if dbg is not None:
            fin.append(("dbgw",))
        if not fin:
            fin = [("ks",)]
        S("sync", lambda e: e.nop(), reads=fin)

        sch.finalize(nc, blk, ctx)
    return nc


def _chunkmajor(W):
    O, Dd = W.shape
    return np.ascontiguousarray(
        W.reshape(O // 128, 128, Dd // 128, 128).transpose(0, 3, 2, 1).reshape(O // 128, 128, Dd))


def _own_tokens(par):
    return np.concatenate([np.arange((8 * t + o) * 128, (8 * t + o) * 128 + 128)
                           for t in range(4) for o in OFFS[par]])


def _masks(par):
    m = np.zeros((8, 128, 512), np.float32)
    tri = (np.arange(128)[:, None] <= np.arange(128)[None, :]).astype(np.float32)
    for r, offs in enumerate((OFFS[par], OFFS[1 - par])):
        for jj, j in enumerate(offs):
            for i, oi in enumerate(OFFS[par]):
                if j < oi:
                    m[r * 4 + jj][:, i * 128:(i + 1) * 128] = 1.0
                elif j == oi:
                    m[r * 4 + jj][:, i * 128:(i + 1) * 128] = tri
    return m


def prep_inputs(inputs):
    x = np.asarray(inputs["L0_attn_x"], np.float32)
    Wq = np.asarray(inputs["L0_attn_Wq"], np.float32)
    Wk = np.asarray(inputs["L0_attn_Wk"], np.float32)
    Wv = np.asarray(inputs["L0_attn_Wv"], np.float32)
    Wo = np.asarray(inputs["L0_attn_Wo"], np.float32)
    Wb = np.asarray(inputs["L0_ffn_W_base"], np.float32)
    bb = np.asarray(inputs["L0_ffn_b_base"], np.float32)
    Wg = np.asarray(inputs["L0_ffn_W_gate"], np.float32)
    Wt = np.asarray(inputs["L0_ffn_W_thr"], np.float32)
    bt = np.asarray(inputs["L0_ffn_b_thr"], np.float32)
    Aw = np.asarray(inputs["L0_ffn_A_w"], np.float32)
    Bw = np.asarray(inputs["L0_ffn_B_w"], np.float32)

    wb4 = _chunkmajor(Wb)
    Bsb = np.ascontiguousarray(Bw.transpose(1, 0, 2).reshape(4096, 128).T)
    wbB = np.ascontiguousarray(np.concatenate(
        [wb4, Bsb.reshape(128, 32, 128).transpose(1, 0, 2)], axis=2))
    Af = Aw.reshape(128, 4096)
    awm = np.ascontiguousarray(Af.reshape(128, 32, 128).transpose(2, 1, 0).reshape(128, 4096))
    gtm = np.zeros((4096, 40), np.float32)
    gtm[:, 0:8] = Wg.T
    gtm[:, 32] = Wt
    shared = dict(
        wbB=wbB, wq4=_chunkmajor(Wq), wk4=_chunkmajor(Wk), wv4=_chunkmajor(Wv),
        wo4=_chunkmajor(Wo), gtm=gtm, awm=awm,
        bbm=np.ascontiguousarray(bb.reshape(32, 128).T),
        bthr=np.ascontiguousarray(bt.reshape(1, 1)),
        idenm=np.eye(128, dtype=np.float32),
        onesrm=np.ones((1, 128), np.float32),
        onescm=np.ones((128, 1), np.float32),
        e16m=np.kron(np.eye(8, dtype=np.float32), np.ones((1, 16), np.float32)))
    pmask = [_masks(0), _masks(1)]
    in_maps = []
    for c in range(8):
        b, par = c // 2, c % 2
        loc = np.concatenate([_own_tokens(par), _own_tokens(1 - par)])
        d = dict(shared)
        d["x"] = np.ascontiguousarray(x[b][loc].T)
        d["maskm"] = pmask[par]
        in_maps.append(d)
    return in_maps


def kernel(**inputs):
    in_maps = prep_inputs(inputs)
    nc = build()
    res = run_bass_kernel_spmd(nc, in_maps, core_ids=list(range(8)))
    out = np.empty((4, 4096, 4096), np.float32)
    for c in range(8):
        b, par = c // 2, c % 2
        out[b][_own_tokens(par)] = np.ascontiguousarray(res.results[c]["out"]).T
    return out
